# version 18
# baseline (speedup 1.0000x reference)
import contextlib
import numpy as np
import ml_dtypes
import concourse.bass as bass
import concourse.mybir as mybir
from concourse.bass import AP
from concourse.bass_utils import run_bass_kernel_spmd
from bass_rust import ReduceOp


F32 = mybir.dt.float32
BF16 = mybir.dt.bfloat16
I32 = mybir.dt.int32
AF = mybir.ActivationFunctionType
ALU = mybir.AluOpType
AX = mybir.AxisListType

S = 2048
D = 1024
NT = S // 128
ND = D // 128
I0 = 2816
NI0 = I0 // 128
H1 = 4096
NH1 = H1 // 128
NHEAD = 16
HDIM = 64
EPS_LN = 1e-5
SOFTMAX_SHIFT = 16.0

ENG2BLOCK = {"sp": "sync", "act": "scalar", "dve": "vector", "pool": "gpsimd", "pe": "tensor"}
PHASE_LIMIT = 99
DBG_NODOWN = False
DBG_YSINGLE = False
DBG_DBANK1 = False
DBG_WQ = None


class Res:
    __slots__ = ("w", "readers")

    def __init__(self):
        self.w = None
        self.readers = []


class Emitter:
    ENGS = ("sp", "act", "pool", "dve", "pe")
    CHAN_GROUPS = {"q0": 6, "q1": 6, "q2": 2}
    CHANS = tuple(f"{g}_{i}" for g, n in CHAN_GROUPS.items() for i in range(n))

    def __init__(self, nc):
        self.nc = nc
        self.cnt = {n: 0 for n in self.ENGS + self.CHANS}
        self.ops = {e: [] for e in self.ENGS}
        self.last_waited = {e: {} for e in self.ENGS}
        self._bar_pending = {e: [] for e in self.ENGS}
        self.chan_uses = {g: 0 for g in self.CHAN_GROUPS}
        self.sems = {}

    def bar(self):
        snap = [(n, self.cnt[n]) for n in self.ENGS + self.CHANS if self.cnt[n] > 0]
        for e in self.ENGS:
            self._bar_pending[e] = [t for t in snap if t[0] != e]

    def op(self, eng, fn, reads=(), writes=(), extra=(), chan=None):
        waits = list(self._bar_pending[eng]) + list(extra)
        self._bar_pending[eng] = []
        for r in reads:
            if r.w is not None:
                waits.append(r.w)
        for r in writes:
            if r.w is not None:
                waits.append(r.w)
            waits.extend(r.readers)
        sem = chan if chan is not None else eng
        amt = 16 if chan is not None else 1
        eff = []
        lw = self.last_waited[eng]
        best = {}
        for s, v in waits:
            if v <= 0:
                continue
            if eng == "pe" and s == "pe":
                continue
            if v > best.get(s, -1):
                best[s] = v
        for s, v in best.items():
            if lw.get(s, -1) >= v:
                continue
            lw[s] = v
            eff.append((s, v))
        self.cnt[sem] += amt
        tok = (sem, self.cnt[sem])
        self.ops[eng].append((eff, fn, sem, amt))
        for r in reads:
            r.readers.append(tok)
        for r in writes:
            r.w = tok
            r.readers = []
        return tok

    def dma(self, eng_chan, out, in_, reads=(), writes=(), extra=()):
        eng, group = eng_chan
        n = self.CHAN_GROUPS[group]
        sem = f"{group}_{self.chan_uses[group] % n}"
        self.chan_uses[group] += 1
        prev = self.cnt[sem]
        extra = list(extra)
        if prev > 0:
            extra.append((sem, prev))
        return self.op(eng, lambda e: e.dma_start(out=out, in_=in_),
                       reads=reads, writes=writes, extra=extra, chan=sem)

    def raw_wait(self, eng, extra):
        waits = list(self._bar_pending[eng]) + list(extra)
        self._bar_pending[eng] = []
        eff = []
        lw = self.last_waited[eng]
        for s, v in waits:
            if v <= 0 or lw.get(s, -1) >= v:
                continue
            lw[s] = v
            eff.append((s, v))
        if eff:
            self.ops[eng].append((eff, None, None, 0))

    def tok(self, sem):
        return (sem, self.cnt[sem])

    def finalize(self, es):
        for n in self.ENGS + self.CHANS:
            self.sems[n] = es.enter_context(self.nc.semaphore("sem_" + n))
        block = es.enter_context(self.nc.Block())
        for eng in self.ENGS:
            entries = self.ops[eng]
            if not entries:
                continue
            section = getattr(block, ENG2BLOCK[eng])

            def make(entries):
                def _(e):
                    for waits, fn, sem, amt in entries:
                        for s, v in waits:
                            e.wait_ge(self.sems[s], v)
                        if fn is not None:
                            fn(e).then_inc(self.sems[sem], amt)
                return _
            section(make(entries))



def emit_ln_tile(em, x_tile, xn_bf_out, mean_col, rstd_col, scr, rx, rxn, rstats):
    sq, tmp, rsq, rtmp = scr["sq"], scr["tmp"], scr["r_sq"], scr["r_tmp"]
    em.op("dve", lambda e: e.tensor_reduce(out=mean_col, in_=x_tile, axis=AX.X, op=ALU.add),
          reads=[rx], writes=[rstats])
    em.op("dve", lambda e: e.tensor_scalar(out=mean_col, in0=mean_col, scalar1=1.0 / D,
                                           scalar2=None, op0=ALU.mult, op1=ALU.bypass),
          reads=[rstats], writes=[rstats])
    em.op("act", lambda e: e.activation(out=sq, in_=x_tile, func=AF.Square),
          reads=[rx], writes=[rsq])
    em.op("dve", lambda e: e.tensor_reduce(out=tmp[:, 0:1], in_=sq, axis=AX.X, op=ALU.add),
          reads=[rsq], writes=[rtmp])
    em.op("dve", lambda e: e.tensor_scalar(out=tmp[:, 0:1], in0=tmp[:, 0:1], scalar1=1.0 / D,
                                           scalar2=None, op0=ALU.mult, op1=ALU.bypass),
          reads=[rtmp], writes=[rtmp])
    em.op("dve", lambda e: e.tensor_scalar(out=tmp[:, 1:2], in0=mean_col, scalar1=mean_col,
                                           scalar2=None, op0=ALU.mult, op1=ALU.bypass),
          reads=[rstats], writes=[rtmp])
    em.op("dve", lambda e: e.scalar_tensor_tensor(out=tmp[:, 0:1], in0=tmp[:, 0:1],
                                                  scalar=EPS_LN, in1=tmp[:, 1:2],
                                                  op0=ALU.add, op1=ALU.subtract),
          reads=[rtmp], writes=[rtmp])
    em.op("act", lambda e: e.activation(out=tmp[:, 0:1], in_=tmp[:, 0:1], func=AF.Ln),
          reads=[rtmp], writes=[rtmp])
    em.op("act", lambda e: e.activation(out=rstd_col, in_=tmp[:, 0:1], func=AF.Exp,
                                        scale=-0.5), reads=[rtmp], writes=[rstats])
    t = em.op("dve", lambda e: e.tensor_scalar(out=xn_bf_out, in0=x_tile, scalar1=mean_col,
                                               scalar2=rstd_col, op0=ALU.subtract,
                                               op1=ALU.mult),
              reads=[rx, rstats], writes=[rxn])
    return t


def emit_fp4_tile(em, xin_f32, qout_bf, f4, r_in, r_out, Wd, pool_chains=False):
    nb = Wd // 16
    A = lambda k: f4[k][0]
    R = lambda k: f4[k][1]
    em.op("act", lambda e: e.activation(out=A("a")[:, 0:Wd], in_=xin_f32, func=AF.Abs),
          reads=[r_in], writes=[R("a")])
    em.op("act", lambda e: e.activation(out=A("sg")[:, 0:Wd], in_=xin_f32, func=AF.Sign),
          reads=[r_in], writes=[R("sg")])
    em.op("dve", lambda e: e.tensor_reduce(
        out=A("bmax")[:, 0:nb].rearrange("p (w o) -> p w o", o=1),
        in_=A("a")[:, 0:Wd].rearrange("p (w s) -> p w s", s=16), axis=AX.X, op=ALU.max),
        reads=[R("a")], writes=[R("bmax")])
    em.op("dve", lambda e: e.tensor_scalar(out=A("bmax")[:, 0:nb], in0=A("bmax")[:, 0:nb],
                                           scalar1=1.0 / 6.0, scalar2=1e-12,
                                           op0=ALU.mult, op1=ALU.max),
          reads=[R("bmax")], writes=[R("bmax")])
    em.op("dve", lambda e: e.tensor_scalar(out=A("hsc")[:, 0:nb], in0=A("bmax")[:, 0:nb],
                                           scalar1=0.5, scalar2=None, op0=ALU.mult,
                                           op1=ALU.bypass),
          reads=[R("bmax")], writes=[R("hsc")])
    em.op("dve", lambda e: e.reciprocal(out=A("inv")[:, 0:nb], in_=A("bmax")[:, 0:nb]),
          reads=[R("bmax")], writes=[R("inv")])
    em.op("dve", lambda e: e.tensor_scalar(out=A("inv")[:, 0:nb], in0=A("inv")[:, 0:nb],
                                           scalar1=4.0, scalar2=None, op0=ALU.mult,
                                           op1=ALU.bypass),
          reads=[R("inv")], writes=[R("inv")])
    bc = lambda t: t[:, 0:nb].rearrange("p (w o) -> p w o", o=1).broadcast_to([128, nb, 16])
    a3 = A("a")[:, 0:Wd].rearrange("p (w s) -> p w s", s=16)
    v3 = A("v")[:, 0:Wd].rearrange("p (w s) -> p w s", s=16)
    s13 = A("s1")[:, 0:Wd].rearrange("p (w s) -> p w s", s=16)
    s23 = A("s2")[:, 0:Wd].rearrange("p (w s) -> p w s", s=16)
    em.op("dve", lambda e: e.tensor_tensor(out=v3, in0=a3, in1=bc(A("inv")),
                                           op=ALU.mult),
          reads=[R("a"), R("inv")], writes=[R("v")])
    if pool_chains:
        em.op("pool", lambda e: e.tensor_scalar(out=A("a")[:, 0:Wd], in0=A("v")[:, 0:Wd],
                                                scalar1=20.0, scalar2=None, op0=ALU.is_gt,
                                                op1=ALU.bypass), reads=[R("v"), R("a")],
              writes=[R("a")])
        em.op("pool", lambda e: e.tensor_tensor(
            out=A("a")[:, 0:Wd], in0=A("a")[:, 0:Wd], in1=A("a")[:, 0:Wd],
            op=ALU.add), reads=[R("a")], writes=[R("a")])
        for thr in (10.0, 14.0):
            em.op("pool", lambda e, thr=thr: e.tensor_scalar(
                out=A("s2")[:, 0:Wd], in0=A("v")[:, 0:Wd], scalar1=thr, scalar2=None,
                op0=ALU.is_gt, op1=ALU.bypass), reads=[R("v")], writes=[R("s2")])
            em.op("pool", lambda e: e.tensor_tensor(
                out=A("a")[:, 0:Wd], in0=A("a")[:, 0:Wd], in1=A("s2")[:, 0:Wd],
                op=ALU.add), reads=[R("a"), R("s2")], writes=[R("a")])
        em.op("pool", lambda e: e.tensor_scalar(out=A("s1")[:, 0:Wd], in0=A("v")[:, 0:Wd],
                                                scalar1=1.0, scalar2=None, op0=ALU.is_gt,
                                                op1=ALU.bypass), reads=[R("v")], writes=[R("s1")])
        for thr in (3.0, 5.0, 7.0):
            em.op("pool", lambda e, thr=thr: e.tensor_scalar(
                out=A("s2")[:, 0:Wd], in0=A("v")[:, 0:Wd], scalar1=thr, scalar2=None,
                op0=ALU.is_gt, op1=ALU.bypass), reads=[R("v")], writes=[R("s2")])
            em.op("pool", lambda e: e.tensor_tensor(
                out=A("s1")[:, 0:Wd], in0=A("s1")[:, 0:Wd], in1=A("s2")[:, 0:Wd],
                op=ALU.add), reads=[R("s1"), R("s2")], writes=[R("s1")])
        em.op("dve", lambda e: e.scalar_tensor_tensor(
            out=A("s1")[:, 0:Wd], in0=A("a")[:, 0:Wd], scalar=2.0, in1=A("s1")[:, 0:Wd],
            op0=ALU.mult, op1=ALU.add), reads=[R("a"), R("s1")], writes=[R("s1")])
    else:
        em.op("dve", lambda e: e.tensor_scalar(out=A("s1")[:, 0:Wd], in0=A("v")[:, 0:Wd],
                                               scalar1=1.0, scalar2=None, op0=ALU.is_gt,
                                               op1=ALU.bypass), reads=[R("v")], writes=[R("s1")])
        for thr in (3.0, 5.0, 7.0):
            em.op("dve", lambda e, thr=thr: e.scalar_tensor_tensor(
                out=A("s1")[:, 0:Wd], in0=A("v")[:, 0:Wd], scalar=thr, in1=A("s1")[:, 0:Wd],
                op0=ALU.is_gt, op1=ALU.add), reads=[R("v"), R("s1")], writes=[R("s1")])
        em.op("dve", lambda e: e.tensor_scalar(out=A("s2")[:, 0:Wd], in0=A("v")[:, 0:Wd],
                                               scalar1=20.0, scalar2=2.0, op0=ALU.is_gt,
                                               op1=ALU.mult), reads=[R("v")], writes=[R("s2")])
        for thr in (10.0, 14.0):
            em.op("dve", lambda e, thr=thr: e.scalar_tensor_tensor(
                out=A("s2")[:, 0:Wd], in0=A("v")[:, 0:Wd], scalar=thr, in1=A("s2")[:, 0:Wd],
                op0=ALU.is_gt, op1=ALU.add), reads=[R("v"), R("s2")], writes=[R("s2")])
    if not pool_chains:
        em.op("dve", lambda e: e.scalar_tensor_tensor(
            out=A("s1")[:, 0:Wd], in0=A("s2")[:, 0:Wd], scalar=2.0, in1=A("s1")[:, 0:Wd],
            op0=ALU.mult, op1=ALU.add), reads=[R("s2"), R("s1")], writes=[R("s1")])
    s3g = A("sg")[:, 0:Wd].rearrange("p (w s) -> p w s", s=16)
    em.op("dve", lambda e: e.tensor_tensor(out=s23, in0=s3g, in1=bc(A("hsc")),
                                           op=ALU.mult),
          reads=[R("sg"), R("hsc"), R("s2")], writes=[R("s2")])
    t = em.op("dve", lambda e: e.tensor_tensor(out=qout_bf, in0=A("s1")[:, 0:Wd],
                                               in1=A("s2")[:, 0:Wd], op=ALU.mult),
              reads=[R("s1"), R("s2")], writes=[r_out])
    return t


def mk_f4(nc, ps, tag, width):
    Sb = lambda n, sh, dt: (ps.enter_context(nc.sbuf_tensor(tag + n, sh, dt)), Res())
    return {"a": Sb("a", [128, width], F32), "v": Sb("v", [128, width], F32),
            "s1": Sb("s1", [128, width], F32), "s2": Sb("s2", [128, width], F32),
            "sg": Sb("sg", [128, width], F32), "bmax": Sb("bm", [128, width // 16], F32),
            "inv": Sb("iv", [128, width // 16], F32), "hsc": Sb("hs", [128, width // 16], F32)}


def emit_transpose_batch4(em, consts, srcs, dsts3d, ptile, r_srcs, r_dst, rp):
    n = len(srcs)
    t = None
    for i, src in enumerate(srcs):
        t = em.op("pe", lambda e, i=i, src=src: e.transpose(
            ptile[:, i * 128:(i + 1) * 128], src, consts["idn_bf"][0][:]),
            reads=[r_srcs[i], consts["idn_bf"][1]], writes=[rp])
    t = em.op("dve", lambda e: e.tensor_copy(
        out=dsts3d, in_=ptile[:, 0:n * 128].rearrange("p (n m) -> p n m", m=128)),
        reads=[rp], writes=[r_dst])
    return t


def emit_transpose_128(em, consts, src_ap, dst_ap, ptile, r_src, r_dst, rp):
    t = em.op("pe", lambda e: e.transpose(ptile[:], src_ap, consts["idn_bf"][0][:]),
              reads=[r_src, consts["idn_bf"][1]], writes=[rp])
    t = em.op("dve", lambda e: e.tensor_copy(out=dst_ap, in_=ptile[:]),
              reads=[rp], writes=[r_dst])
    return t



def build_program(layers):
    nc = bass.Bass("TRN2", target_bir_lowering=False)
    em = Emitter(nc)
    es = contextlib.ExitStack()

    dr = {}
    dr["x_in"] = nc.dram_tensor("x_in", [S, D], F32, kind="ExternalInput")
    dr["x_out"] = nc.dram_tensor("x_out", [S, D], F32, kind="ExternalOutput")
    if 0 in layers:
        dr["W0"] = nc.dram_tensor("W0", [D, D], F32, kind="ExternalInput")
        dr["b0"] = nc.dram_tensor("b0", [D], F32, kind="ExternalInput")
        dr["wg"] = nc.dram_tensor("wg", [I0, D], F32, kind="ExternalInput")
        dr["wu"] = nc.dram_tensor("wu", [I0, D], F32, kind="ExternalInput")
        dr["wd"] = nc.dram_tensor("wd", [D, I0], F32, kind="ExternalInput")
        dr["wdTq"] = nc.dram_tensor("wdTq", [I0, D], BF16, kind="Internal")
        dr["wgTq"] = nc.dram_tensor("wgTq", [D, I0], BF16, kind="Internal")
        dr["wuTq"] = nc.dram_tensor("wuTq", [D, I0], BF16, kind="Internal")
        dr["xqq"] = nc.dram_tensor("xqq", [D, S], BF16, kind="Internal")
    if 1 in layers:
        dr["Wqkv"] = nc.dram_tensor("Wqkv", [D, 3 * D], F32, kind="ExternalInput")
        dr["bqkv"] = nc.dram_tensor("bqkv", [3 * D], F32, kind="ExternalInput")
        dr["Wout"] = nc.dram_tensor("Wout", [D, D], F32, kind="ExternalInput")
        dr["bout"] = nc.dram_tensor("bout", [D], F32, kind="ExternalInput")
        dr["W1T"] = nc.dram_tensor("W1T", [D, H1], F32, kind="ExternalInput")
        dr["b1"] = nc.dram_tensor("b1", [H1], F32, kind="ExternalInput")
        dr["W2T"] = nc.dram_tensor("W2T", [H1, D], F32, kind="ExternalInput")
        dr["b2"] = nc.dram_tensor("b2", [D], F32, kind="ExternalInput")
        dr["W1q"] = nc.dram_tensor("W1q", [NH1 * ND * 128, 128], BF16, kind="Internal")
        dr["W2q"] = nc.dram_tensor("W2q", [H1, D], BF16, kind="Internal")
        dr["xsave"] = nc.dram_tensor("xsave", [S, D], F32, kind="Internal")

    SbR = lambda *a: (es.enter_context(nc.sbuf_tensor(*a)), Res())
    x_tm = es.enter_context(nc.sbuf_tensor("x_tm", [128, NT * D], F32))
    rx = [Res() for _ in range(NT)]
    consts = {}
    consts["idn_bf"] = SbR("idn_bf", [128, 128], BF16)
    consts["idn_f32"] = SbR("idn_f32", [128, 128], F32)
    consts["ones_col_bf"] = SbR("ones_col_bf", [128, 1], BF16)
    consts["ones_row_bf"] = SbR("ones_row_bf", [1, 128], BF16)
    consts["ones_row_f32"] = SbR("ones_row_f32", [1, 128], F32)
    consts["iota_col"] = SbR("iota_col", [128, 1], F32)

    stats_m = es.enter_context(nc.sbuf_tensor("stats_m", [128, NT], F32))
    stats_r = es.enter_context(nc.sbuf_tensor("stats_r", [128, NT], F32))
    r_stats = Res()

    em.op("pool", lambda e: e.iota(consts["idn_bf"][0][:], pattern=[[-1, 128]], base=0,
                                   channel_multiplier=1, allow_small_or_imprecise_dtypes=True),
          writes=[consts["idn_bf"][1]])
    em.op("pool", lambda e: e.tensor_scalar(out=consts["idn_bf"][0][:],
                                            in0=consts["idn_bf"][0][:], scalar1=0.0,
                                            scalar2=None, op0=ALU.is_equal, op1=ALU.bypass),
          reads=[consts["idn_bf"][1]], writes=[consts["idn_bf"][1]])
    em.op("pool", lambda e: e.tensor_copy(out=consts["idn_f32"][0][:],
                                          in_=consts["idn_bf"][0][:]),
          reads=[consts["idn_bf"][1]], writes=[consts["idn_f32"][1]])
    em.op("pool", lambda e: e.memset(consts["ones_col_bf"][0][:], 1.0),
          writes=[consts["ones_col_bf"][1]])
    em.op("pool", lambda e: e.memset(consts["ones_row_bf"][0][:], 1.0),
          writes=[consts["ones_row_bf"][1]])
    em.op("pool", lambda e: e.memset(consts["ones_row_f32"][0][:], 1.0),
          writes=[consts["ones_row_f32"][1]])
    em.op("pool", lambda e: e.iota(consts["iota_col"][0][:], pattern=[[0, 1]], base=0,
                                   channel_multiplier=1, allow_small_or_imprecise_dtypes=True),
          writes=[consts["iota_col"][1]])
    em.bar()
    em.dma(("sp", "q0"), x_tm[:].rearrange("p (t d) -> p t d", t=NT),
           dr["x_in"].ap().rearrange("(t p) d -> p t d", p=128), writes=rx)
    em.bar()

    if 0 in layers:
        emit_layer0(nc, em, es, dr, consts, x_tm, rx,
                    stats_m, stats_r, r_stats, layer_x_dram=dr["x_in"])
    if 1 in layers:
        emit_layer1(nc, em, es, dr, consts, x_tm, rx, stats_m, stats_r, r_stats)

    em.bar()
    t_out = em.dma(("sp", "q0"), dr["x_out"].ap().rearrange("(t p) d -> p t d", p=128),
                   x_tm[:].rearrange("p (t d) -> p t d", t=NT), reads=rx)
    em.raw_wait("sp", [t_out])
    em.finalize(es)
    es.close()
    return nc



def emit_layer0(nc, em, es, dr, consts, x_tm, rx, stats_m, stats_r, r_stats,
                layer_x_dram):
    with contextlib.ExitStack() as ps:
        Sb = lambda *a: ps.enter_context(nc.sbuf_tensor(*a))
        Pb = lambda *a: ps.enter_context(nc.psum_tensor(*a))
        xn_tm = Sb("l0_xn_tm", [128, NT * D], BF16)
        r_xn = [Res() for _ in range(NT)]
        xn_fm = Sb("l0_xn_fm", [128, ND * S], BF16)
        r_xnf = Res()
        xtrb = Sb("l0_xtrb", [128, ND * 512], BF16)
        r_xtr = [Res()]
        r_xtr.append(r_xtr[0])
        probsT = Sb("l0_probsT", [128, NT * 512], BF16)
        _rpb = [Res() for _ in range(NT)]
        r_pb = [_rpb, _rpb]
        W_b = Sb("l0_W_b", [128, ND * D], BF16)
        r_W = Res()
        wstg = Sb("l0_wstg", [128, 2 * D], F32)
        r_ws = [Res(), Res()]
        b_col = Sb("l0_b_col", [128, ND], F32)
        r_bc = Res()
        sq = Sb("l0_sq", [128, D], F32)
        tmp1 = Sb("l0_tmp1", [128, 4], F32)
        ln_scr_rsq_holder = [Res()]
        sca = Sb("l0_sca", [128, 512], F32)
        scb = sca
        r_sca = Res()
        r_scb = r_sca
        f4 = mk_f4(nc, ps, "l0q", 512)
        qtile = Sb("l0q_qt", [128, 512], BF16)
        r_qt = Res()
        obuf = Sb("l0q_ob", [128, 512], BF16)
        r_ob = [Res()]
        r_ob.append(r_ob[0])
        xf = wstg[:, 0:D]
        r_xf = r_ws[0]
        xnf = sq
        r_xnf2 = ln_scr_rsq_holder[0]
        peps = [Pb("l0_peps0", [128, 512], BF16), Pb("l0_peps1", [128, 512], BF16)]
        r_pp = [Res(), Res()]
        ps_sc = Pb("l0_ps_sc", [128, 512], F32)
        ps_sc2 = Pb("l0_ps_sc2", [128, 512], F32)
        r_psc, r_psc2 = Res(), Res()
        ps_pv = Pb("l0_ps_pv", [128, 1024], F32)
        r_ppv = Res()
        ps_sum = Pb("l0_ps_sum", [128, 1], F32)
        r_psum = Res()
        ps_b = Pb("l0_ps_b", [128, 8], F32)
        r_psb = Res()

        lw_bf = Sb("l0_lw_bf", [128, 4224], BF16)
        r_lw = Res()
        ln_scr = {"sq": sq[:, 0:D], "tmp": tmp1, "r_sq": ln_scr_rsq_holder[0], "r_tmp": Res()}
        for sl0 in range(0, 4224, 1024):
            w0 = min(1024, 4224 - sl0)
            em.op("pool", lambda e, sl0=sl0, w0=w0: e.iota(
                wstg[:, 0:w0], pattern=[[-1, w0]], base=2048 - sl0,
                channel_multiplier=1, allow_small_or_imprecise_dtypes=True),
                writes=[r_ws[0]])
            em.op("act", lambda e, w0=w0: e.activation(out=wstg[:, 0:w0],
                                                       in_=wstg[:, 0:w0], func=AF.Abs),
                  reads=[r_ws[0]], writes=[r_ws[0]])
            em.op("dve", lambda e, sl0=sl0, w0=w0: e.tensor_scalar(
                out=lw_bf[:, sl0:sl0 + w0], in0=wstg[:, 0:w0], scalar1=0.5, scalar2=None,
                op0=ALU.is_gt, op1=ALU.bypass), reads=[r_ws[0]], writes=[r_lw])
            em.op("dve", lambda e, w0=w0: e.tensor_scalar(
                out=wstg[:, 0:w0], in0=wstg[:, 0:w0], scalar1=1e-7, scalar2=None,
                op0=ALU.add, op1=ALU.bypass), reads=[r_ws[0]], writes=[r_ws[0]])
            em.op("dve", lambda e, w0=w0: e.reciprocal(
                out=wstg[:, 1024:1024 + w0], in_=wstg[:, 0:w0]),
                reads=[r_ws[0]], writes=[r_ws[1]])
            em.op("dve", lambda e, sl0=sl0, w0=w0: e.tensor_tensor(
                out=lw_bf[:, sl0:sl0 + w0], in0=lw_bf[:, sl0:sl0 + w0],
                in1=wstg[:, 1024:1024 + w0], op=ALU.mult),
                reads=[r_ws[1], r_lw], writes=[r_lw])
        for t in range(NT):
            emit_ln_tile(em, x_tm[:, t * D:(t + 1) * D], xn_tm[:, t * D:(t + 1) * D],
                         stats_m[:, t:t + 1], stats_r[:, t:t + 1], ln_scr,
                         rx[t], r_xn[t], r_stats)
        if PHASE_LIMIT <= 2:
            return
        xnf3 = xn_fm[:].rearrange("p (c s) -> p c s", c=ND)
        for t in range(NT):
            for g in range(2):
                emit_transpose_batch4(
                    em, consts,
                    [xn_tm[:, t * D + (g * 4 + i) * 128: t * D + (g * 4 + i + 1) * 128]
                     for i in range(4)],
                    xnf3[:, g * 4: g * 4 + 4, t * 128:(t + 1) * 128],
                    peps[g], [r_xn[t]] * 4, r_xnf, r_pp[g])
        if PHASE_LIMIT <= 3:
            return
        t_bl = em.dma(("sp", "q0"), wstg[0:8, 0:128],
                      dr["b0"].ap().rearrange("(c p) -> c p", p=128), writes=[r_ws[0]])
        em.op("pe", lambda e: e.transpose(ps_b[:, 0:8], wstg[0:8, 0:128],
                                          consts["idn_f32"][0][0:8, 0:8]),
              reads=[r_ws[0], consts["idn_f32"][1]], writes=[r_psb])
        em.op("dve", lambda e: e.tensor_copy(out=b_col[:], in_=ps_b[:, 0:8]),
              reads=[r_psb], writes=[r_bc])
        for dc in range(ND):
            sl = dc % 2
            em.dma(("sp", "q0"), wstg[:, sl * D: sl * D + D],
                   dr["W0"].ap()[dc * 128:(dc + 1) * 128, :], writes=[r_ws[sl]])
            em.op("dve", lambda e, dc=dc, sl=sl: e.tensor_copy(
                out=W_b[:, dc * D:(dc + 1) * D], in_=wstg[:, sl * D: sl * D + D]),
                reads=[r_ws[sl]], writes=[r_W])
        if PHASE_LIMIT <= 4:
            return
        def emit_xtran_scores(qb):
            for dcp in range(ND):
                pp, rp = (ps_sc, r_psc) if dcp % 2 == 0 else (ps_sc2, r_psc2)
                for dc in range(ND):
                    em.op("pe", lambda e, dc=dc, dcp=dcp, qb=qb, pp=pp: e.matmul(
                        pp[:, 0:512],
                        lhsT=W_b[:, dc * D + dcp * 128: dc * D + (dcp + 1) * 128],
                        rhs=xn_fm[:, dc * S + qb * 512: dc * S + qb * 512 + 512],
                        start=(dc == 0), stop=(dc == ND - 1)),
                        reads=[r_W, r_xnf], writes=[rp])
                em.op("dve", lambda e, dcp=dcp, pp=pp, qb=qb: e.tensor_scalar(
                    out=xtrb[:, dcp * 512:(dcp + 1) * 512],
                    in0=pp[:, 0:512], scalar1=b_col[:, dcp:dcp + 1], scalar2=None,
                    op0=ALU.add, op1=ALU.bypass),
                    reads=[rp, r_bc], writes=[r_xtr[qb % 2]])
            for tc in range(NT):
                pp, rp = (ps_sc, r_psc) if tc % 2 == 0 else (ps_sc2, r_psc2)
                sc, rsc = (sca, r_sca) if tc % 2 == 0 else (scb, r_scb)
                for dc in range(ND):
                    em.op("pe", lambda e, dc=dc, tc=tc, qb=qb, pp=pp: e.matmul(
                        pp[:, 0:512],
                        lhsT=xn_fm[:, dc * S + tc * 128: dc * S + (tc + 1) * 128],
                        rhs=xtrb[:, dc * 512:(dc + 1) * 512],
                        start=(dc == 0), stop=(dc == ND - 1)),
                        reads=[r_xnf, r_xtr[qb % 2]], writes=[rp])
                c0 = 2048 - tc * 128 + qb * 512
                em.op("dve", lambda e, pp=pp, c0=c0, sc=sc: e.tensor_tensor(
                    out=sc[:, 0:512], in0=pp[:, 0:512], in1=lw_bf[:, c0:c0 + 512],
                    op=ALU.mult), reads=[rp, r_lw], writes=[rsc])
                em.op("act", lambda e, sc=sc: e.activation(out=sc[:, 0:512],
                                                           in_=sc[:, 0:512], func=AF.Tanh),
                      reads=[rsc], writes=[rsc])
                em.op("act", lambda e, sc=sc, tc=tc, qb=qb: e.activation(
                    out=probsT[:, tc * 512:(tc + 1) * 512],
                    in_=sc[:, 0:512], func=AF.Exp),
                    reads=[rsc], writes=[r_pb[qb % 2][tc]])
                if qb * 4 <= tc < qb * 4 + 4:
                    off = tc * 128 - qb * 512
                    base = tc * 512 + off
                    em.op("dve", lambda e, base=base: e.tensor_tensor(
                        out=probsT[:, base: base + 128],
                        in0=probsT[:, base: base + 128],
                        in1=consts["idn_bf"][0][:], op=ALU.subtract),
                        reads=[r_pb[qb % 2][tc], consts["idn_bf"][1]],
                        writes=[r_pb[qb % 2][tc]])

        def emit_pv(qb):
            for sub in range(4):
                qt = qb * 4 + sub
                for tc in range(NT):
                    base = tc * 512 + sub * 128
                    em.op("pe", lambda e, base=base, tc=tc: e.matmul(
                        ps_pv[:, 0:512],
                        lhsT=probsT[:, base: base + 128],
                        rhs=xn_tm[:, tc * D: tc * D + 512],
                        start=(tc == 0), stop=(tc == NT - 1)),
                        reads=[r_pb[qb % 2][tc], r_xn[tc]], writes=[r_ppv])
                    em.op("pe", lambda e, base=base, tc=tc: e.matmul(
                        ps_pv[:, 512:1024],
                        lhsT=probsT[:, base: base + 128],
                        rhs=xn_tm[:, tc * D + 512: tc * D + 1024],
                        start=(tc == 0), stop=(tc == NT - 1)),
                        reads=[r_pb[qb % 2][tc], r_xn[tc]], writes=[r_ppv])
                    em.op("pe", lambda e, base=base, tc=tc: e.matmul(
                        ps_sum[:, 0:1],
                        lhsT=probsT[:, base: base + 128],
                        rhs=consts["ones_col_bf"][0][:],
                        start=(tc == 0), stop=(tc == NT - 1)),
                        reads=[r_pb[qb % 2][tc], consts["ones_col_bf"][1]],
                        writes=[r_psum])
                r_tmp = ln_scr["r_tmp"]
                em.op("dve", lambda e: e.tensor_scalar(
                    out=tmp1[:, 0:1], in0=ps_sum[:, 0:1], scalar1=1e-7, scalar2=None,
                    op0=ALU.add, op1=ALU.bypass), reads=[r_psum], writes=[r_tmp])
                em.op("dve", lambda e: e.reciprocal(out=tmp1[:, 1:2], in_=tmp1[:, 0:1]),
                      reads=[r_tmp], writes=[r_tmp])
                for half in range(2):
                    em.op("dve", lambda e, qt=qt, half=half: e.scalar_tensor_tensor(
                        out=x_tm[:, qt * D + half * 512: qt * D + half * 512 + 512],
                        in0=ps_pv[:, half * 512:(half + 1) * 512], scalar=tmp1[:, 1:2],
                        in1=x_tm[:, qt * D + half * 512: qt * D + half * 512 + 512],
                        op0=ALU.mult, op1=ALU.add),
                        reads=[r_ppv, r_tmp, rx[qt]], writes=[rx[qt]])

        tq = peps
        r_tq = r_pp

        def quant_store_row(wdr_ap, src_f32, r_src, dst_dram, dst_base_fn, width):
            for s0 in range(0, width, 512):
                w0 = min(512, width - s0)
                emit_fp4_tile(em, src_f32[:, s0:s0 + w0], qtile[:, 0:w0], f4,
                              r_src, r_qt, w0)
                nn = w0 // 128
                g = 0
                gg = (s0 // 512) % 2
                emit_transpose_batch4(
                    em, consts,
                    [qtile[:, i * 128:(i + 1) * 128] for i in range(nn)],
                    obuf[:, 0:nn * 128].rearrange("p (n m) -> p n m", m=128),
                    tq[gg], [r_qt] * nn, r_ob[0], r_tq[gg])
                off, rstride = dst_base_fn(s0 // 128)
                em.dma(("pool", "q1"),
                       AP(dst_dram, off, [[rstride, 128], [128 * rstride, nn], [1, 128]]),
                       obuf[:, 0:nn * 128].rearrange("p (n m) -> p n m", m=128),
                       reads=[r_ob[0]])

        def prep_units():
            loads = []
            for wname, oname in (("wg", "wgTq"), ("wu", "wuTq")):
                for rc in range(NI0):
                    loads.append((wname, rc * 128, 0, D,
                                  dr[oname],
                                  lambda ci, rc=rc: (ci * 128 * I0 + rc * 128, I0)))
            for rc in range(ND):
                for cs in range(3):
                    c0 = cs * 1024
                    cw = min(1024, I0 - c0)
                    loads.append(("wd", rc * 128, c0, cw, dr["wdTq"],
                                  lambda ci, rc=rc, c0=c0: ((c0 // 128 + ci) * 128 * D + rc * 128, D)))

            def issue_load(k):
                wname, r0, c0, cw, _, _ = loads[k]
                sl = k % 2
                em.dma(("pool", "q1"), wstg[:, D + sl * D: D + sl * D + cw] if False else
                       wstg[:, sl * D: sl * D + cw],
                       dr[wname].ap()[r0:r0 + 128, c0:c0 + cw], writes=[r_ws[sl]])

            issue_load(0)
            for k in range(len(loads)):
                if k + 1 < len(loads):
                    issue_load(k + 1)
                _, _, _, cw, dst, base_fn = loads[k]
                sl = k % 2
                quant_store_row(None, wstg[:, sl * D: sl * D + cw], r_ws[sl], dst,
                                base_fn, cw)
                yield
            for t in range(NT):
                em.dma(("pool", "q1"), xf,
                       layer_x_dram.ap()[t * 128:(t + 1) * 128, :], writes=[r_xf])
                em.op("dve", lambda e, t=t: e.tensor_scalar(
                    out=xnf[:, 0:D], in0=xf,
                    scalar1=stats_m[:, t:t + 1], scalar2=stats_r[:, t:t + 1],
                    op0=ALU.subtract, op1=ALU.mult),
                    reads=[r_xf, r_stats], writes=[r_xnf2])
                quant_store_row(None, xnf[:, 0:D], r_xnf2, dr["xqq"],
                                lambda ci, t=t: (ci * 128 * S + t * 128, S), D)
                yield

        gen = prep_units()

        def step_gen(n=1):
            for _ in range(n):
                if next(gen, "DONE") == "DONE":
                    return

        for qb in range(4):
            emit_xtran_scores(qb)
            step_gen(5)
            emit_pv(qb)
            step_gen(13)
        step_gen(100)
        em.bar()
    if PHASE_LIMIT <= 5:
        return
    emit_layer0_ffn(nc, em, es, dr, consts, x_tm, rx, stats_m, stats_r, r_stats,
                    layer_x_dram)


def emit_layer0_ffn(nc, em, es, dr, consts, x_tm, rx, stats_m, stats_r, r_stats,
                    layer_x_dram):
    with contextlib.ExitStack() as ps:
        Sb = lambda *a: ps.enter_context(nc.sbuf_tensor(*a))
        Pb = lambda *a: ps.enter_context(nc.psum_tensor(*a))
        xq_fm = Sb("f0_xq_fm", [128, ND * S], BF16)
        r_xq = Res()
        HW = 1408
        NIH = HW // 128
        wgT = Sb("f0_wgT", [128, ND * HW], BF16)
        wuT = Sb("f0_wuT", [128, ND * HW], BF16)
        r_wg, r_wu = Res(), Res()
        hq = Sb("f0_hq", [128, 2 * NIH * 256], BF16)
        r_hq = [[[Res() for _ in range(2)] for _ in range(NIH)] for _ in range(2)]
        wdres = Sb("f0_wdres", [128, NIH * 1024], BF16)
        r_wd = [Res() for _ in range(NIH)]
        f4 = mk_f4(nc, ps, "f0", 512)
        qtile = Sb("f0_qt", [128, 2 * 512], BF16)
        r_qts = [Res(), Res()]
        hsc_f = Sb("f0_hscf", [128, 512], F32)
        hf = Sb("f0_hf", [128, 512], F32)
        r_hs, r_hf = Res(), Res()
        peps = [Pb("f0_peps0", [128, 512], BF16), Pb("f0_peps1", [128, 512], BF16)]
        r_pp = [Res(), Res()]
        ps_g = [Pb("f0_ps_g0", [128, 512], F32), Pb("f0_ps_g1", [128, 512], F32)]
        r_pg = [Res(), Res()]
        ps_y = [Pb("f0_ps_y0", [128, 512], F32), Pb("f0_ps_y1", [128, 512], F32)]
        r_py = [Res(), Res()]
        ps_d = [Pb("f0_ps_d0", [128, 512], F32), Pb("f0_ps_d1", [128, 512], F32)]
        r_pd = [Res() for _ in range(2)]

        if PHASE_LIMIT <= 6:
            return
        em.bar()
        for dc in range(ND):
            em.dma(("sp", "q0" if dc % 2 == 0 else "q2"),
                   xq_fm[:, dc * S:(dc + 1) * S],
                   dr["xqq"].ap()[dc * 128:(dc + 1) * 128, :], writes=[r_xq])
        if PHASE_LIMIT <= 7:
            return
        em.bar()
        iblocks = [(0, 512), (512, 512), (1024, 384)]
        hq3 = hq[:].rearrange("p (b c s) -> p b c s", b=2, c=NIH)

        def emit_gateyh(half, ssb):
            buf = ssb % 2
            pend = []
            qct = [0]

            def flush_pend():
                while pend:
                    pend.pop(0)()
            for st in range(2):
                qt = ssb * 2 + st
                for ibi, (i0b, iw) in enumerate(iblocks):
                    pg, rpg = ps_g[ibi % 2], r_pg[ibi % 2]
                    for dc in range(ND):
                        em.op("pe", lambda e, dc=dc, qt=qt, i0b=i0b, iw=iw, pg=pg: e.matmul(
                            pg[:, 0:iw],
                            lhsT=xq_fm[:, dc * S + qt * 128: dc * S + (qt + 1) * 128],
                            rhs=wgT[:, dc * HW + i0b: dc * HW + i0b + iw],
                            start=(dc == 0), stop=(dc == ND - 1)),
                            reads=[r_xq, r_wg], writes=[rpg])
                    py, rpy = ((ps_y[0], r_py[0]) if DBG_YSINGLE else (ps_y[ibi % 2], r_py[ibi % 2]))
                    for dc in range(ND):
                        em.op("pe", lambda e, dc=dc, qt=qt, i0b=i0b, iw=iw, py=py: e.matmul(
                            py[:, 0:iw],
                            lhsT=xq_fm[:, dc * S + qt * 128: dc * S + (qt + 1) * 128],
                            rhs=wuT[:, dc * HW + i0b: dc * HW + i0b + iw],
                            start=(dc == 0), stop=(dc == ND - 1)),
                            reads=[r_xq, r_wu], writes=[rpy])
                    em.op("act", lambda e, iw=iw, pg=pg: e.activation(
                        out=hsc_f[:, 0:iw], in_=pg[:, 0:iw], func=AF.Silu),
                        reads=[rpg], writes=[r_hs])
                    em.op("dve", lambda e, iw=iw, py=py: e.tensor_tensor(
                        out=hf[:, 0:iw], in0=hsc_f[:, 0:iw], in1=py[:, 0:iw],
                        op=ALU.mult), reads=[r_hs, rpy], writes=[r_hf])
                    qslot = qct[0] % 2
                    qct[0] += 1
                    flush_pend()
                    emit_fp4_tile(em, hf[:, 0:iw], qtile[:, qslot * 512: qslot * 512 + iw],
                                  f4, r_hf, r_qts[qslot], iw)

                    def mk_tp(buf=buf, st=st, i0b=i0b, iw=iw, qslot=qslot):
                        def go():
                            nn = iw // 128
                            for g in range((nn + 3) // 4):
                                cnt = min(4, nn - g * 4)
                                ic0 = i0b // 128 + g * 4
                                emit_transpose_batch4(
                                    em, consts,
                                    [qtile[:, qslot * 512 + (g * 4 + i) * 128:
                                           qslot * 512 + (g * 4 + i + 1) * 128]
                                     for i in range(cnt)],
                                    hq3[:, buf, ic0:ic0 + cnt, st * 128:(st + 1) * 128],
                                    peps[g % 2], [r_qts[qslot]] * cnt,
                                    r_hq[buf][ic0][st], r_pp[g % 2])
                                for ii in range(1, cnt):
                                    r_hq[buf][ic0 + ii][st].w = r_hq[buf][ic0][st].w
                                    r_hq[buf][ic0 + ii][st].readers = r_hq[buf][ic0][st].readers
                        return go
                    pend.append(mk_tp())
            flush_pend()

        def emit_down(half, ssb):
            buf = ssb % 2
            for db in range(2):
                for st in range(2):
                    bk = 0 if DBG_DBANK1 else (db * 2 + st) % 2
                    for ici in range(NIH):
                        em.op("pe", lambda e, ici=ici, st=st, db=db,
                              buf=buf, bk=bk: e.matmul(
                            ps_d[bk][:, 0:512],
                            lhsT=hq3[:, buf, ici, st * 128:(st + 1) * 128],
                            rhs=wdres[:, ici * 1024 + db * 512: ici * 1024 + db * 512 + 512],
                            start=(ici == 0), stop=(ici == NIH - 1)),
                            reads=[r_hq[buf][ici][st], r_wd[ici]],
                            writes=[r_pd[bk]])
                    qt = ssb * 2 + st
                    em.op("dve", lambda e, qt=qt, db=db, bk=bk: e.tensor_tensor(
                        out=x_tm[:, qt * D + db * 512: qt * D + db * 512 + 512],
                        in0=x_tm[:, qt * D + db * 512: qt * D + db * 512 + 512],
                        in1=ps_d[bk][:, 0:512],
                        op=ALU.add),
                        reads=[r_pd[bk], rx[qt]], writes=[rx[qt]])

        for half in range(2):
            h0 = half * HW
            for dc in range(ND):
                em.dma(("sp", "q1"), wgT[:, dc * HW:(dc + 1) * HW],
                       dr["wgTq"].ap()[dc * 128:(dc + 1) * 128, h0:h0 + HW],
                       writes=[r_wg])
                em.dma(("sp", "q0"), wuT[:, dc * HW:(dc + 1) * HW],
                       dr["wuTq"].ap()[dc * 128:(dc + 1) * 128, h0:h0 + HW],
                       writes=[r_wu])
            for ici in range(NIH):
                em.dma(("pool", "q1" if ici % 2 == 0 else "q2"),
                       wdres[:, ici * 1024:(ici + 1) * 1024],
                       dr["wdTq"].ap()[(half * NIH + ici) * 128:
                                       (half * NIH + ici + 1) * 128, :],
                       writes=[r_wd[ici]])
            for ssb in range(8):
                emit_gateyh(half, ssb)
                if ssb > 0 and not DBG_NODOWN:
                    emit_down(half, ssb - 1)
            if not DBG_NODOWN:
                emit_down(half, 7)
        em.bar()


ALIBI_SLOPES = [2.0 ** (-0.5 * i) for i in range(NHEAD)]


def emit_layer1(nc, em, es, dr, consts, x_tm, rx, stats_m, stats_r, r_stats):
    with contextlib.ExitStack() as outer:
        aofm = outer.enter_context(nc.sbuf_tensor("l1_aofm", [128, ND * S], BF16))
        r_aofm = [Res() for _ in range(ND)]
        emit_layer1_attn(nc, em, outer, dr, consts, x_tm, rx, stats_m, stats_r, r_stats,
                         aofm, r_aofm)
        emit_layer1_wout(nc, em, outer, dr, consts, x_tm, rx, aofm, r_aofm)
    if PHASE_LIMIT <= 12:
        return
    emit_layer1_ffn(nc, em, es, dr, consts, x_tm, rx, stats_m, stats_r, r_stats)


def emit_layer1_attn(nc, em, es, dr, consts, x_tm, rx, stats_m, stats_r, r_stats,
                     aofm, r_aofm):
    with contextlib.ExitStack() as ps:
        Sb = lambda *a: ps.enter_context(nc.sbuf_tensor(*a))
        Pb = lambda *a: ps.enter_context(nc.psum_tensor(*a))
        xn_fm = Sb("l1_xn_fm", [128, ND * S], BF16)
        r_xnf = Res()
        xn_t = Sb("l1_xn_t", [128, D], BF16)
        r_xnt = [Res()]
        sq = Sb("l1_sq", [128, D], F32)
        tmp1 = Sb("l1_tmp1", [128, 4], F32)
        wqs = Sb("l1_wqs", [128, 2 * 128], F32)
        r_wqs = [Res(), Res()]
        wq_b = Sb("l1_wq_b", [128, 2 * 3 * ND * 128], BF16)
        r_wqb = [Res(), Res()]
        qkv_fm = Sb("l1_qkv_fm", [128, 3 * S], BF16)
        r_qkv = Res()
        Qp = Sb("l1_Qp", [68, S], BF16)
        Qm = Sb("l1_Qm", [68, S], BF16)
        Kx = Sb("l1_Kx", [68, S], BF16)
        r_Qp, r_Qm, r_Kx = Res(), Res(), Res()
        V1 = Sb("l1_V1", [128, NT * 65], BF16)
        r_V1 = Res()
        probsT = Sb("l1_probsT", [128, NT * 512], BF16)
        r_pb = [Res() for _ in range(NT)]
        wh = Sb("l1_wh", [128, 896], BF16)
        bandc = Sb("l1_bandc", [128, 896], F32)
        r_wh, r_bandc = Res(), Res()
        biasm = Sb("l1_biasm", [128, 1], F32)
        r_bm = Res()
        iota_row = Sb("l1_iotar", [1, S], F32)
        r_ior = Res()

        srow = sq[0:1, 0:512]
        rrep = sq[0:64, 512:1024]
        r_srow = Res()
        r_rrep = Res()
        exps = xn_t[:, 0:512]
        rampB = Sb("l1_rampB", [1, S // 2], BF16)
        r_rsB = Res()
        r_exps = [Res()]
        r_exps.append(r_exps[0])
        bq_col = Sb("l1_bq_col", [128, 24], F32)
        r_bqc = Res()
        peps = [Pb("l1_peps0", [128, 512], BF16)]
        peps.append(peps[0])
        r_pp = [Res()]
        r_pp.append(r_pp[0])
        ps_a = Pb("l1_ps_a", [128, 1024], F32)
        ps_b2 = Pb("l1_ps_b2", [128, 1024], F32)
        r_psa, r_psb = Res(), Res()
        ps_pv = [Pb("l1_ps_pv0", [128, 512], F32), Pb("l1_ps_pv1", [128, 512], F32)]
        ps_rep = Pb("l1_ps_rep", [64, 512], F32)
        r_prep = Res()
        r_ppv = [Res(), Res()]
        r_tmp = Res()

        em.bar()
        ln_scr = {"sq": sq[:, 0:D], "tmp": tmp1, "r_sq": Res(), "r_tmp": r_tmp}
        em.dma(("sp", "q0"), dr["xsave"].ap().rearrange("(t p) d -> p t d", p=128),
               x_tm[:].rearrange("p (t d) -> p t d", t=NT), reads=rx)
        for t in range(NT):
            emit_ln_tile(em, x_tm[:, t * D:(t + 1) * D], xn_t[:, 0:D],
                         stats_m[:, t:t + 1], stats_r[:, t:t + 1], ln_scr,
                         rx[t], r_xnt[0], r_stats)
            xnf3 = xn_fm[:].rearrange("p (c s) -> p c s", c=ND)
            for g in range(2):
                emit_transpose_batch4(
                    em, consts,
                    [xn_t[:, (g * 4 + i) * 128:(g * 4 + i + 1) * 128] for i in range(4)],
                    xnf3[:, g * 4: g * 4 + 4, t * 128:(t + 1) * 128],
                    peps[g], [r_xnt[0]] * 4, r_xnf, r_pp[g])
        em.op("pool", lambda e: e.iota(iota_row[:], pattern=[[1, S]], base=0,
                                       channel_multiplier=0,
                                       allow_small_or_imprecise_dtypes=True),
              writes=[r_ior])
        em.op("pool", lambda e: e.memset(biasm[:, 0:1], -SOFTMAX_SHIFT), writes=[r_bm])
        em.op("pool", lambda e: e.iota(bandc[:], pattern=[[-1, 896]], base=384,
                                       channel_multiplier=1,
                                       allow_small_or_imprecise_dtypes=True),
              writes=[r_bandc])
        em.op("act", lambda e: e.activation(out=bandc[:], in_=bandc[:], func=AF.Abs),
              reads=[r_bandc], writes=[r_bandc])
        em.dma(("sp", "q0"), sq[0:24, 0:128],
               dr["bqkv"].ap().rearrange("(c p) -> c p", p=128), writes=[ln_scr["r_sq"]])
        em.op("pe", lambda e: e.transpose(ps_a[:, 0:24], sq[0:24, 0:128],
                                          consts["idn_f32"][0][0:24, 0:24]),
              reads=[ln_scr["r_sq"], consts["idn_f32"][1]], writes=[r_psa])
        em.op("dve", lambda e: e.tensor_copy(out=bq_col[:], in_=ps_a[:, 0:24]),
              reads=[r_psa], writes=[r_bqc])
        em.bar()

        def load_wqb(pair, B):
            for kind in range(3):
                for dc in range(ND):
                    sl = dc % 2
                    m0k = kind * D + pair * 128
                    em.dma(("sp", "q0"), wqs[:, sl * 128: sl * 128 + 128],
                           dr["Wqkv"].ap()[dc * 128:(dc + 1) * 128, m0k:m0k + 128],
                           writes=[r_wqs[sl]])
                    em.op("dve", lambda e, kind=kind, dc=dc, sl=sl, B=B: e.tensor_copy(
                        out=wq_b[:, (B * 3 * ND + kind * ND + dc) * 128:
                                 (B * 3 * ND + kind * ND + dc + 1) * 128],
                        in_=wqs[:, sl * 128: sl * 128 + 128]),
                        reads=[r_wqs[sl]], writes=[r_wqb[B]])

        load_wqb(0, 0)
        for pair in range(8):
            B = pair % 2
            for kind in range(3):
                for sb in range(4):
                    pp, rp = (ps_a, r_psa) if sb % 2 == 0 else (ps_b2, r_psb)
                    for dc in range(ND):
                        em.op("pe", lambda e, kind=kind, dc=dc, sb=sb, pp=pp,
                              B=B: e.matmul(
                            pp[:, 0:512],
                            lhsT=wq_b[:, (B * 3 * ND + kind * ND + dc) * 128:
                                      (B * 3 * ND + kind * ND + dc + 1) * 128],
                            rhs=xn_fm[:, dc * S + sb * 512: dc * S + sb * 512 + 512],
                            start=(dc == 0), stop=(dc == ND - 1)),
                            reads=[r_wqb[B], r_xnf], writes=[rp])
                    em.op("act", lambda e, kind=kind, sb=sb, pp=pp, pair=pair: e.activation(
                        out=qkv_fm[:, kind * S + sb * 512: kind * S + sb * 512 + 512],
                        in_=pp[:, 0:512], func=AF.Identity,
                        bias=bq_col[:, kind * 8 + pair: kind * 8 + pair + 1]),
                        reads=[rp, r_bqc], writes=[r_qkv])
            if pair + 1 < 8:
                load_wqb(pair + 1, (pair + 1) % 2)
            for half in range(2):
                h = 2 * pair + half
                slope = ALIBI_SLOPES[h]
                em.op("pool", lambda e: e.memset(V1[:], 1.0), writes=[r_V1])
                V13 = V1[:].rearrange("p (t c) -> p t c", c=65)
                for tg in range(4):
                    q = tg % 2
                    for i in range(4):
                        tc = tg * 4 + i
                        em.op("pe", lambda e, half=half, tc=tc, q=q, i=i: e.transpose(
                            peps[q][0:128, i * 64:(i + 1) * 64],
                            qkv_fm[64 * half:64 * half + 64,
                                   2 * S + tc * 128: 2 * S + (tc + 1) * 128],
                            consts["idn_bf"][0][64 * half:64 * half + 64,
                                                64 * half:64 * half + 64]),
                            reads=[r_qkv, consts["idn_bf"][1]], writes=[r_pp[q]])
                    em.op("dve", lambda e, tg=tg, q=q, V13=V13: e.tensor_copy(
                        out=V13[:, tg * 4: tg * 4 + 4, 0:64],
                        in_=peps[q][0:128, 0:256].rearrange("p (n m) -> p n m", m=64)),
                        reads=[r_pp[q]], writes=[r_V1])
                em.op("dve", lambda e, half=half: e.tensor_copy(
                    out=Qp[0:64, :], in_=qkv_fm[64 * half:64 * half + 64, 0:S]),
                    reads=[r_qkv], writes=[r_Qp])
                em.op("dve", lambda e, half=half: e.tensor_copy(
                    out=Qm[0:64, :], in_=qkv_fm[64 * half:64 * half + 64, 0:S]),
                    reads=[r_qkv], writes=[r_Qm])
                em.op("dve", lambda e, half=half: e.tensor_copy(
                    out=Kx[0:64, :], in_=qkv_fm[64 * half:64 * half + 64, S:2 * S]),
                    reads=[r_qkv], writes=[r_Kx])
                em.op("pool", lambda e: e.memset(Qp[64:68, :], 1.0),
                      reads=[r_Qp], writes=[r_Qp])
                for sgn, Qx, rQ in ((1.0, Qp, r_Qp), (-1.0, Qm, r_Qm)):
                    if sgn < 0:
                        em.op("pool", lambda e: e.memset(Qm[64:68, :], -1.0),
                              reads=[r_Qm], writes=[r_Qm])
                    for hh in range(2):
                        c0, c1 = hh * 1024, (hh + 1) * 1024
                        em.op("dve", lambda e, slope=slope, sgn=sgn, c0=c0, c1=c1:
                              e.tensor_scalar(
                            out=rampB[:], in0=iota_row[0:1, c0:c1],
                            scalar1=sgn * 8.0 * slope, scalar2=None,
                            op0=ALU.mult, op1=ALU.bypass),
                            reads=[r_ior], writes=[r_rsB])
                        em.dma(("sp", "q0"), Qx[64:65, c0:c1], rampB[:],
                               reads=[r_rsB], writes=[rQ])
                        em.op("dve", lambda e, slope=slope, sgn=sgn, c0=c0, c1=c1:
                              e.scalar_tensor_tensor(
                            out=rampB[:], in0=iota_row[0:1, c0:c1],
                            scalar=sgn * 8.0 * slope, in1=rampB[:],
                            op0=ALU.mult, op1=ALU.subtract),
                            reads=[r_ior], writes=[r_rsB])
                        em.dma(("sp", "q0"), Qx[65:66, c0:c1], rampB[:],
                               reads=[r_rsB], writes=[rQ])
                        if sgn < 0:
                            em.dma(("sp", "q0"), Kx[67:68, c0:c1], rampB[:],
                                   reads=[r_rsB], writes=[r_Kx])
                em.op("pool", lambda e: e.memset(Kx[64:66, :], 1.0),
                      reads=[r_Kx], writes=[r_Kx])
                em.dma(("sp", "q0"), Kx[66:67, :], Qm[64:65, :], reads=[r_Qm],
                       writes=[r_Kx])

                em.op("act", lambda e, slope=slope: e.activation(
                    out=wh[:], in_=bandc[:], func=AF.Exp, scale=-slope),
                    reads=[r_bandc], writes=[r_wh])
                def emit_norm(qb):
                    pv, rpv = ps_pv[qb % 2], r_ppv[qb % 2]
                    em.op("dve", lambda e, pv=pv: e.tensor_copy(
                        out=srow, in_=pv[64:65, 0:512]),
                        reads=[rpv, ln_scr["r_sq"]], writes=[r_srow])
                    em.op("pe", lambda e: e.matmul(
                        ps_rep[0:64, 0:512], lhsT=consts["ones_row_f32"][0][0:1, 0:64],
                        rhs=srow, start=True, stop=True),
                        reads=[r_srow, consts["ones_row_f32"][1]], writes=[r_prep])
                    em.op("dve", lambda e: e.reciprocal(
                        out=rrep, in_=ps_rep[0:64, 0:512]),
                        reads=[r_prep, ln_scr["r_sq"]], writes=[r_rrep])
                    em.op("dve", lambda e, pv=pv, h=h, qb=qb, half=half: e.tensor_tensor(
                        out=aofm[64 * half:64 * half + 64,
                                 (h // 2) * S + qb * 512:(h // 2) * S + qb * 512 + 512],
                        in0=pv[0:64, 0:512], in1=rrep, op=ALU.mult),
                        reads=[rpv, r_rrep], writes=[r_aofm[h // 2]])

                for qb in range(4):
                    lo = qb * 4
                    pairs = [(t, t + 1) for t in range(0, lo, 2)] + \
                            [(t, t + 1) for t in range(lo + 4, NT, 2)]
                    pcnt = 0
                    for (t0, t1) in pairs:
                        Qx, rQx = (Qm, r_Qm) if (t1 + 1) * 128 <= qb * 512 + 512 \
                            else (Qp, r_Qp)
                        pp, rp = (ps_a, r_psa) if pcnt % 2 == 0 else (ps_b2, r_psb)
                        pcnt += 1
                        for i, tc in enumerate((t0, t1)):
                            em.op("pe", lambda e, tc=tc, qb=qb, pp=pp, Qx=Qx,
                                  i=i: e.matmul(
                                pp[:, i * 512:(i + 1) * 512],
                                lhsT=Kx[0:68, tc * 128:(tc + 1) * 128],
                                rhs=Qx[0:68, qb * 512: qb * 512 + 512],
                                start=True, stop=True),
                                reads=[r_Kx, rQx], writes=[rp])
                        em.op("act", lambda e, t0=t0, pp=pp: e.activation(
                            out=probsT[:, t0 * 512:(t0 + 2) * 512], in_=pp[:, 0:1024],
                            func=AF.Exp, scale=0.125, bias=biasm[:, 0:1]),
                            reads=[rp, r_bm], writes=[r_pb[t0], r_pb[t1]])
                    for tc in range(lo, lo + 4):
                        pp, rp = (ps_a, r_psa) if pcnt % 2 == 0 else (ps_b2, r_psb)
                        pcnt += 1
                        em.op("pe", lambda e, tc=tc, qb=qb, pp=pp: e.matmul(
                            pp[:, 0:512],
                            lhsT=Kx[0:64, tc * 128:(tc + 1) * 128],
                            rhs=Qp[0:64, qb * 512: qb * 512 + 512],
                            start=True, stop=True),
                            reads=[r_Kx, r_Qp], writes=[rp])
                        em.op("act", lambda e, pp=pp: e.activation(
                            out=exps[:, 0:512], in_=pp[:, 0:512],
                            func=AF.Exp, scale=0.125, bias=biasm[:, 0:1]),
                            reads=[rp, r_bm], writes=[r_exps[0]])
                        c0 = 384 - tc * 128 + qb * 512
                        em.op("dve", lambda e, tc=tc, c0=c0: e.tensor_tensor(
                            out=probsT[:, tc * 512:(tc + 1) * 512],
                            in0=exps[:, 0:512],
                            in1=wh[:, c0:c0 + 512], op=ALU.mult),
                            reads=[r_exps[0], r_wh], writes=[r_pb[tc]])
                    pv, rpv = ps_pv[qb % 2], r_ppv[qb % 2]
                    for tc in range(NT):
                        em.op("pe", lambda e, tc=tc, pv=pv: e.matmul(
                            pv[0:65, 0:512],
                            lhsT=V1[:, tc * 65:(tc + 1) * 65],
                            rhs=probsT[:, tc * 512:(tc + 1) * 512],
                            start=(tc == 0), stop=(tc == NT - 1)),
                            reads=[r_pb[tc], r_V1], writes=[rpv])
                    if qb > 0:
                        emit_norm(qb - 1)
                emit_norm(3)
        em.bar()


def emit_layer1_wout(nc, em, es, dr, consts, x_tm, rx, aofm, r_aofm):
    with contextlib.ExitStack() as ps:
        Sb = lambda *a: ps.enter_context(nc.sbuf_tensor(*a))
        Pb = lambda *a: ps.enter_context(nc.psum_tensor(*a))
        wout_b = Sb("l1_wout_b", [128, ND * D], BF16)
        r_wout = Res()
        wstg = Sb("l1_wstg", [128, 2 * D], F32)
        r_ws = [Res(), Res()]
        brow = Sb("l1_brow", [1, D], BF16)
        brow_f = Sb("l1_brow_f", [1, D], F32)
        r_brow = Res()
        ps_a = Pb("l1w_ps_a", [128, 512], F32)
        ps_b2 = Pb("l1w_ps_b2", [128, 512], F32)
        r_psa, r_psb = Res(), Res()
        em.bar()
        for dc in range(ND):
            sl = dc % 2
            em.dma(("sp", "q0"), wstg[:, sl * D: sl * D + D],
                   dr["Wout"].ap()[dc * 128:(dc + 1) * 128, :], writes=[r_ws[sl]])
            em.op("dve", lambda e, dc=dc, sl=sl: e.tensor_copy(
                out=wout_b[:, dc * D:(dc + 1) * D], in_=wstg[:, sl * D: sl * D + D]),
                reads=[r_ws[sl]], writes=[r_wout])
        em.dma(("sp", "q0"), brow_f[0:1, 0:D],
               dr["bout"].ap().rearrange("(a d) -> a d", a=1), writes=[r_brow])
        em.op("dve", lambda e: e.tensor_copy(out=brow[0:1, 0:D], in_=brow_f[0:1, 0:D]),
              reads=[r_brow], writes=[r_brow])
        em.bar()
        for t in range(NT):
            for db in range(2):
                pp, rp = (ps_a, r_psa) if db == 0 else (ps_b2, r_psb)
                for dc in range(ND):
                    em.op("pe", lambda e, dc=dc, db=db, pp=pp, t=t: e.matmul(
                        pp[:, 0:512],
                        lhsT=aofm[:, dc * S + t * 128: dc * S + (t + 1) * 128],
                        rhs=wout_b[:, dc * D + db * 512: dc * D + db * 512 + 512],
                        start=(dc == 0), stop=False),
                        reads=[r_aofm[dc], r_wout], writes=[rp])
                em.op("pe", lambda e, db=db, pp=pp: e.matmul(
                    pp[:, 0:512], lhsT=consts["ones_row_bf"][0][:],
                    rhs=brow[0:1, db * 512: db * 512 + 512],
                    start=False, stop=True),
                    reads=[consts["ones_row_bf"][1], r_brow], writes=[rp])
                em.op("dve", lambda e, t=t, db=db, pp=pp: e.tensor_tensor(
                    out=x_tm[:, t * D + db * 512: t * D + db * 512 + 512],
                    in0=x_tm[:, t * D + db * 512: t * D + db * 512 + 512],
                    in1=pp[:, 0:512], op=ALU.add),
                    reads=[rp, rx[t]], writes=[rx[t]])
        em.bar()


def emit_layer1_ffn(nc, em, es, dr, consts, x_tm, rx, stats_m, stats_r, r_stats):
    with contextlib.ExitStack() as outer:
        wsv = outer.enter_context(nc.sbuf_tensor("f1_wsv", [128, 2], F32))
        r_wsv = Res()
        emit_l1ffn_prep(nc, em, outer, dr, consts, wsv, r_wsv)
        if PHASE_LIMIT <= 13:
            return
        emit_l1ffn_main(nc, em, outer, dr, consts, x_tm, rx, stats_m, stats_r, r_stats,
                        wsv, r_wsv)


def emit_l1ffn_prep(nc, em, es, dr, consts, wsv, r_wsv):
    with contextlib.ExitStack() as ps:
        Sb = lambda *a: ps.enter_context(nc.sbuf_tensor(*a))
        wstg = Sb("f1p_wstg", [128, 8 * 2048], F32)
        r_ws = [Res() for _ in range(8)]
        tscr = Sb("f1p_tscr", [128, 2048], F32)
        r_tscr = Res()
        tbf = Sb("f1p_tbf", [128, 2 * 2048], BF16)
        r_tbf = [Res(), Res()]
        acc = Sb("f1p_acc", [128, 8], F32)
        r_acc = Res()
        thr = Sb("f1p_thr", [128, 4], F32)
        r_thr = Res()
        onesf = Sb("f1p_onesf", [128, 1], F32)
        r_of = Res()
        onesr = Sb("f1p_onesr", [1, 128], F32)
        r_or = Res()
        srow = Sb("f1p_srow", [1, 1], F32)
        r_sr = Res()
        pss = ps.enter_context(nc.psum_tensor("f1p_pss", [128, 1], F32))
        r_psx = Res()
        em.bar()
        em.op("pool", lambda e: e.memset(onesf[:], 1.0), writes=[r_of])
        em.op("pool", lambda e: e.memset(onesr[:], 1.0), writes=[r_or])
        pieces = {0: [], 1: []}
        for wi, (wname, nchunk, width) in enumerate((("W1T", ND, H1), ("W2T", NH1, D))):
            npiece = max(width // 2048, 1)
            pw = min(width, 2048)
            for rc in range(nchunk):
                for pc in range(npiece):
                    pieces[wi].append((wname, rc, pc, pw))

        def pass1(wi):
            first = True
            plist = pieces[wi]

            def issue(k):
                wname, rc, pc, pw = plist[k]
                sl = k % 4
                em.dma(("sp", "q0" if k % 2 == 0 else "q1"),
                       wstg[:, sl * 2048: sl * 2048 + pw],
                       dr[wname].ap()[rc * 128:(rc + 1) * 128,
                                      pc * 2048: pc * 2048 + pw],
                       writes=[r_ws[sl]])
            for k in range(min(4, len(plist))):
                issue(k)
            for k, (wname, rc, pc, pw) in enumerate(plist):
                sl = k % 4
                em.op("dve", lambda e, sl=sl, pw=pw: e.tensor_reduce(
                    out=acc[:, 1:2], in_=wstg[:, sl * 2048: sl * 2048 + pw],
                    axis=AX.X, op=ALU.add, apply_absolute_value=True),
                    reads=[r_ws[sl]], writes=[r_acc])
                if k + 4 < len(plist):
                    issue(k + 4)
                if first:
                    em.op("dve", lambda e: e.tensor_copy(out=acc[:, 0:1], in_=acc[:, 1:2]),
                          reads=[r_acc], writes=[r_acc])
                    first = False
                else:
                    em.op("dve", lambda e: e.tensor_tensor(
                        out=acc[:, 0:1], in0=acc[:, 0:1], in1=acc[:, 1:2], op=ALU.add),
                        reads=[r_acc], writes=[r_acc])
            em.op("pe", lambda e: e.matmul(pss[0:1, 0:1], lhsT=acc[:, 0:1],
                                           rhs=onesf[:, 0:1], start=True, stop=True),
                  reads=[r_acc, r_of], writes=[r_psx])
            em.op("dve", lambda e: e.tensor_copy(out=srow[0:1, 0:1], in_=pss[0:1, 0:1]),
                  reads=[r_psx], writes=[r_sr])
            em.op("pe", lambda e: e.matmul(pss[:, 0:1], lhsT=onesr[0:1, 0:128],
                                           rhs=srow[0:1, 0:1], start=True, stop=True),
                  reads=[r_sr, r_or], writes=[r_psx])
            em.op("dve", lambda e: e.tensor_copy(out=acc[:, 2:3], in_=pss[:, 0:1]),
                  reads=[r_psx], writes=[r_acc])
            em.op("dve", lambda e, wi=wi: e.tensor_scalar(
                out=wsv[:, wi:wi + 1], in0=acc[:, 2:3], scalar1=1.0 / (D * H1),
                scalar2=1e-5, op0=ALU.mult, op1=ALU.max),
                reads=[r_acc], writes=[r_wsv])
            em.op("dve", lambda e, wi=wi: e.tensor_scalar(
                out=thr[:, 2 * wi: 2 * wi + 1], in0=wsv[:, wi:wi + 1], scalar1=0.5,
                scalar2=None, op0=ALU.mult, op1=ALU.bypass),
                reads=[r_wsv], writes=[r_thr])
            em.op("dve", lambda e, wi=wi: e.tensor_scalar(
                out=thr[:, 2 * wi + 1: 2 * wi + 2], in0=thr[:, 2 * wi: 2 * wi + 1],
                scalar1=-1.0, scalar2=None, op0=ALU.mult, op1=ALU.bypass),
                reads=[r_thr], writes=[r_thr])

        def pass2(wi):
            plist = pieces[wi]

            def issue(k):
                wname, rc, pc, pw = plist[k]
                sl = 4 + k % 4
                em.dma(("sp", "q0" if k % 2 == 0 else "q1"),
                       wstg[:, sl * 2048: sl * 2048 + pw],
                       dr[wname].ap()[rc * 128:(rc + 1) * 128,
                                      pc * 2048: pc * 2048 + pw],
                       writes=[r_ws[sl]])
            for k in range(min(4, len(plist))):
                issue(k)
            for k, (wname, rc, pc, pw) in enumerate(plist):
                sl = 4 + k % 4
                tb = k % 2
                em.op("dve", lambda e, sl=sl, pw=pw, wi=wi: e.tensor_scalar(
                    out=tscr[:, 0:pw], in0=wstg[:, sl * 2048: sl * 2048 + pw],
                    scalar1=thr[:, 2 * wi: 2 * wi + 1], scalar2=None,
                    op0=ALU.is_gt, op1=ALU.bypass),
                    reads=[r_ws[sl], r_thr], writes=[r_tscr])
                em.op("dve", lambda e, sl=sl, pw=pw, wi=wi, tb=tb: e.scalar_tensor_tensor(
                    out=tbf[:, tb * 2048: tb * 2048 + pw],
                    in0=wstg[:, sl * 2048: sl * 2048 + pw],
                    scalar=thr[:, 2 * wi + 1: 2 * wi + 2], in1=tscr[:, 0:pw],
                    op0=ALU.is_lt, op1=ALU.subtract),
                    reads=[r_ws[sl], r_thr, r_tscr], writes=[r_tbf[tb]])
                if k + 4 < len(plist):
                    issue(k + 4)
                if wi == 0:
                    off = ((pc * 16 * ND + rc) * 128) * 128
                    em.dma(("pool", "q2"),
                           AP(dr["W1q"], off,
                              [[128, 128], [ND * 128 * 128, 16], [1, 128]]),
                           tbf[:, tb * 2048: tb * 2048 + 2048]
                           .rearrange("p (h c) -> p h c", c=128),
                           reads=[r_tbf[tb]])
                else:
                    em.dma(("pool", "q2"), dr["W2q"].ap()[rc * 128:(rc + 1) * 128, :],
                           tbf[:, tb * 2048: tb * 2048 + D], reads=[r_tbf[tb]])

        pass1(0)
        pass1(1)
        pass2(0)
        pass2(1)
        em.bar()


def emit_l1ffn_main(nc, em, es, dr, consts, x_tm, rx, stats_m, stats_r, r_stats,
                    wsv, r_wsv):
    with contextlib.ExitStack() as ps:
        Sb = lambda *a: ps.enter_context(nc.sbuf_tensor(*a))
        Pb = lambda *a: ps.enter_context(nc.psum_tensor(*a))
        xq_fm = Sb("f1_xq_fm", [128, ND * S], BF16)
        r_xq = Res()
        h_sb = Sb("f1_h_sb", [128, NH1 * 512], BF16)
        r_h = [Res() for _ in range(NH1)]
        w1s = Sb("f1_w1s", [128, 4 * ND * 128], BF16)
        r_w1s = [Res() for _ in range(4)]
        w2s = Sb("f1_w2s", [128, 4 * D], BF16)
        r_w2s = [Res() for _ in range(4)]
        xf = Sb("f1_xf", [128, 2 * D], F32)
        r_xf = [Res(), Res()]
        xnf = Sb("f1_xnf", [128, D], F32)
        r_xnf = Res()
        qi = Sb("f1_qi", [128, D], I32)
        r_qi = Res()
        qtile = Sb("f1_qt", [128, D], BF16)
        r_qt = Res()
        xs_row = Sb("f1_xs_row", [1, S], F32)
        r_xsr = Res()
        xsc_rep = Sb("f1_xsc_rep", [128, S], F32)
        r_xscr = Res()
        xsc1 = Sb("f1_xsc1", [128, 4], F32)
        r_x1 = Res()
        b1c = Sb("f1_b1c", [128, NH1], F32)
        r_b1 = Res()
        b2r = Sb("f1_b2r", [128, D], F32)
        r_b2 = Res()
        bstg = Sb("f1_bstg", [128, 128], F32)
        r_bs = Res()
        brf = Sb("f1_brf", [1, D], F32)
        r_br = Res()
        Mx = Sb("f1_Mx", [128, 512], F32)
        Mn = Sb("f1_Mn", [128, 512], F32)
        r_Mx, r_Mn = Res(), Res()
        xs2 = Sb("f1_xs2", [128, 512], F32)
        r_xs2 = Res()
        inv2 = Sb("f1_inv2", [128, 512], F32)
        r_iv2 = Res()
        uscr = Sb("f1_uscr", [128, 512], F32)
        r_us = Res()
        qi2 = Sb("f1_qi2", [128, 512], I32)
        r_qi2 = Res()
        sc2 = Sb("f1_sc2", [128, 8], F32)
        r_sc2 = Res()
        wfac = Sb("f1_wfac", [128, 2], F32)
        r_wf = Res()
        onesr = Sb("f1_onesr", [1, 128], F32)
        r_or = Res()
        peps = [Pb("f1_peps0", [128, 512], BF16)]
        peps.append(peps[0])
        r_pp = [Res()]
        r_pp.append(r_pp[0])
        pepf = Pb("f1_pepf", [128, 128], F32)
        r_pf = Res()
        ps_h = [Pb("f1_ps_h0", [128, 512], F32), Pb("f1_ps_h1", [128, 512], F32)]
        r_ph = [Res(), Res()]
        ps_o = [Pb("f1_ps_o0", [128, 512], F32), Pb("f1_ps_o1", [128, 512], F32),
                Pb("f1_ps_o2", [128, 512], F32), Pb("f1_ps_o3", [128, 512], F32)]
        r_po = [Res() for _ in range(4)]

        em.bar()
        em.op("pool", lambda e: e.memset(onesr[:], 1.0), writes=[r_or])
        em.op("dve", lambda e: e.tensor_scalar(out=wfac[:], in0=wsv[:],
                                               scalar1=-1.0 / 127.0, scalar2=None,
                                               op0=ALU.mult, op1=ALU.bypass),
              reads=[r_wsv], writes=[r_wf])
        em.dma(("sp", "q0"), bstg[0:NH1, 0:128],
               dr["b1"].ap().rearrange("(c p) -> c p", p=128), writes=[r_bs])
        em.op("pe", lambda e: e.transpose(pepf[:, 0:NH1], bstg[0:NH1, 0:128],
                                          consts["idn_f32"][0][0:NH1, 0:NH1]),
              reads=[r_bs, consts["idn_f32"][1]], writes=[r_pf])
        em.op("dve", lambda e: e.tensor_copy(out=b1c[:], in_=pepf[:, 0:NH1]),
              reads=[r_pf], writes=[r_b1])
        em.dma(("sp", "q0"), brf[0:1, 0:D],
               dr["b2"].ap().rearrange("(a d) -> a d", a=1), writes=[r_br])
        for _pc in range(2):
            em.op("pe", lambda e, _pc=_pc: e.matmul(
                ps_o[0][:, 0:512], lhsT=onesr[0:1, 0:128],
                rhs=brf[0:1, _pc * 512:(_pc + 1) * 512], start=True, stop=True),
                reads=[r_br, r_or], writes=[r_po[0]])
            em.op("dve", lambda e, _pc=_pc: e.tensor_copy(
                out=b2r[:, _pc * 512:(_pc + 1) * 512], in_=ps_o[0][:, 0:512]),
                reads=[r_po[0]], writes=[r_b2])
        for t in range(NT):
            sl = t % 2
            em.dma(("sp", "q0"), xf[:, sl * D: sl * D + D],
                   dr["xsave"].ap()[t * 128:(t + 1) * 128, :], writes=[r_xf[sl]])
            em.op("dve", lambda e, t=t, sl=sl: e.tensor_scalar(
                out=xnf[:, 0:D], in0=xf[:, sl * D: sl * D + D],
                scalar1=stats_m[:, t:t + 1], scalar2=stats_r[:, t:t + 1],
                op0=ALU.subtract, op1=ALU.mult),
                reads=[r_xf[sl], r_stats], writes=[r_xnf])
            em.op("dve", lambda e: e.tensor_reduce(
                out=xsc1[:, 0:1], in_=xnf[:, 0:D], axis=AX.X, op=ALU.max,
                apply_absolute_value=True), reads=[r_xnf], writes=[r_x1])
            em.op("dve", lambda e: e.tensor_scalar(
                out=xsc1[:, 1:2], in0=xsc1[:, 0:1], scalar1=1e-5, scalar2=None,
                op0=ALU.max, op1=ALU.bypass), reads=[r_x1], writes=[r_x1])
            em.op("dve", lambda e: e.reciprocal(out=xsc1[:, 2:3], in_=xsc1[:, 1:2]),
                  reads=[r_x1], writes=[r_x1])
            em.op("dve", lambda e: e.tensor_scalar(
                out=xsc1[:, 3:4], in0=xsc1[:, 2:3], scalar1=127.0, scalar2=None,
                op0=ALU.mult, op1=ALU.bypass), reads=[r_x1], writes=[r_x1])
            em.op("dve", lambda e: e.tensor_scalar(
                out=qi[:, 0:D], in0=xnf[:, 0:D], scalar1=xsc1[:, 3:4], scalar2=None,
                op0=ALU.mult, op1=ALU.bypass), reads=[r_xnf, r_x1], writes=[r_qi])
            em.op("dve", lambda e: e.tensor_copy(out=qtile[:, 0:D], in_=qi[:, 0:D]),
                  reads=[r_qi], writes=[r_qt])
            xqf3 = xq_fm[:].rearrange("p (c s) -> p c s", c=ND)
            for g in range(2):
                emit_transpose_batch4(
                    em, consts,
                    [qtile[:, (g * 4 + i) * 128:(g * 4 + i + 1) * 128] for i in range(4)],
                    xqf3[:, g * 4: g * 4 + 4, t * 128:(t + 1) * 128],
                    peps[g], [r_qt] * 4, r_xq, r_pp[g])
            em.op("pe", lambda e: e.transpose(pepf[0:1, 0:128], xsc1[:, 1:2],
                                              consts["idn_f32"][0][:]),
                  reads=[r_x1, consts["idn_f32"][1]], writes=[r_pf])
            em.op("dve", lambda e, t=t: e.tensor_copy(
                out=xs_row[0:1, t * 128:(t + 1) * 128], in_=pepf[0:1, 0:128]),
                reads=[r_pf], writes=[r_xsr])
        em.op("dve", lambda e: e.tensor_scalar(
            out=xs_row[0:1, :], in0=xs_row[0:1, :], scalar1=wfac[0:1, 0:1], scalar2=None,
            op0=ALU.mult, op1=ALU.bypass), reads=[r_xsr, r_wf], writes=[r_xsr])
        for _pc in range(4):
            em.op("pe", lambda e, _pc=_pc: e.matmul(
                ps_o[0][:, 0:512], lhsT=onesr[0:1, 0:128],
                rhs=xs_row[0:1, _pc * 512:(_pc + 1) * 512], start=True, stop=True),
                reads=[r_xsr, r_or], writes=[r_po[0]])
            em.op("dve", lambda e, _pc=_pc: e.tensor_copy(
                out=xsc_rep[:, _pc * 512:(_pc + 1) * 512], in_=ps_o[0][:, 0:512]),
                reads=[r_po[0]], writes=[r_xscr])
        em.bar()
        if PHASE_LIMIT <= 14:
            return
        for sb in range(4):
            for hc in range(NH1):
                sl = hc % 4
                em.dma(("sp", "q0" if hc % 2 == 0 else "q1"),
                       w1s[:, sl * ND * 128:(sl + 1) * ND * 128],
                       dr["W1q"].ap()[hc * ND * 128:(hc + 1) * ND * 128, :]
                       .rearrange("(c p) m -> p c m", p=128), writes=[r_w1s[sl]])
                php, rph = ps_h[hc % 2], r_ph[hc % 2]
                for dc in range(ND):
                    em.op("pe", lambda e, hc=hc, dc=dc, sb=sb, sl=sl, php=php: e.matmul(
                        php[:, 0:512],
                        lhsT=w1s[:, (sl * ND + dc) * 128:(sl * ND + dc + 1) * 128],
                        rhs=xq_fm[:, dc * S + sb * 512: dc * S + sb * 512 + 512],
                        start=(dc == 0), stop=(dc == ND - 1)),
                        reads=[r_w1s[sl], r_xq], writes=[rph])
                em.op("dve", lambda e, sb=sb, php=php: e.tensor_tensor(
                    out=uscr[:, 0:512], in0=php[:, 0:512],
                    in1=xsc_rep[:, sb * 512:(sb + 1) * 512], op=ALU.mult),
                    reads=[rph, r_xscr], writes=[r_us])
                em.op("act", lambda e, hc=hc: e.activation(
                    out=h_sb[:, hc * 512:(hc + 1) * 512], in_=uscr[:, 0:512],
                    func=AF.Gelu, bias=b1c[:, hc:hc + 1]),
                    reads=[r_us, r_b1], writes=[r_h[hc]])
                if hc == 0:
                    em.op("dve", lambda e: e.tensor_copy(out=Mx[:], in_=h_sb[:, 0:512]),
                          reads=[r_h[0]], writes=[r_Mx])
                    em.op("dve", lambda e: e.tensor_copy(out=Mn[:], in_=h_sb[:, 0:512]),
                          reads=[r_h[0]], writes=[r_Mn])
                else:
                    em.op("dve", lambda e, hc=hc: e.tensor_tensor(
                        out=Mx[:], in0=Mx[:], in1=h_sb[:, hc * 512:(hc + 1) * 512],
                        op=ALU.max), reads=[r_Mx, r_h[hc]], writes=[r_Mx])
                    em.op("dve", lambda e, hc=hc: e.tensor_tensor(
                        out=Mn[:], in0=Mn[:], in1=h_sb[:, hc * 512:(hc + 1) * 512],
                        op=ALU.min), reads=[r_Mn, r_h[hc]], writes=[r_Mn])
            em.op("dve", lambda e: e.tensor_scalar(
                out=Mn[:], in0=Mn[:], scalar1=-1.0, scalar2=None, op0=ALU.mult,
                op1=ALU.bypass), reads=[r_Mn], writes=[r_Mn])
            em.op("dve", lambda e: e.tensor_tensor(out=xs2[:], in0=Mx[:], in1=Mn[:],
                                                   op=ALU.max),
                  reads=[r_Mx, r_Mn], writes=[r_xs2])
            em.op("dve", lambda e: e.tensor_scalar(
                out=xs2[:], in0=xs2[:], scalar1=1e-5, scalar2=None, op0=ALU.max,
                op1=ALU.bypass), reads=[r_xs2], writes=[r_xs2])
            for st in range(4):
                em.op("pe", lambda e, st=st: e.transpose(
                    pepf[:, 0:128], xs2[:, st * 128:(st + 1) * 128],
                    consts["idn_f32"][0][:]),
                    reads=[r_xs2, consts["idn_f32"][1]], writes=[r_pf])
                em.op("dve", lambda e, st=st: e.tensor_reduce(
                    out=sc2[:, 4 + st: 5 + st], in_=pepf[:, 0:128], axis=AX.X,
                    op=ALU.max), reads=[r_pf], writes=[r_sc2])
                em.op("dve", lambda e, st=st: e.tensor_scalar(
                    out=sc2[:, st:st + 1], in0=sc2[:, 4 + st: 5 + st],
                    scalar1=wfac[:, 1:2], scalar2=None, op0=ALU.mult, op1=ALU.bypass),
                    reads=[r_sc2, r_wf], writes=[r_sc2])
                em.op("dve", lambda e, st=st: e.reciprocal(
                    out=sc2[:, 4 + st: 5 + st], in_=sc2[:, 4 + st: 5 + st]),
                    reads=[r_sc2], writes=[r_sc2])
                em.op("pe", lambda e, st=st: e.transpose(
                    pepf[0:1, 0:128], sc2[:, 4 + st: 5 + st],
                    consts["idn_f32"][0][:]),
                    reads=[r_sc2, consts["idn_f32"][1]], writes=[r_pf])
                em.op("dve", lambda e, st=st: e.tensor_scalar(
                    out=xs_row[0:1, st * 128:(st + 1) * 128], in0=pepf[0:1, 0:128],
                    scalar1=127.0, scalar2=None, op0=ALU.mult, op1=ALU.bypass),
                    reads=[r_pf], writes=[r_xsr])
            em.op("pe", lambda e: e.matmul(
                ps_o[0][:, 0:512], lhsT=onesr[0:1, 0:128], rhs=xs_row[0:1, 0:512],
                start=True, stop=True), reads=[r_xsr, r_or], writes=[r_po[0]])
            em.op("dve", lambda e: e.tensor_copy(out=inv2[:], in_=ps_o[0][:, 0:512]),
                  reads=[r_po[0]], writes=[r_iv2])
            em.op("dve", lambda e: e.tensor_copy(out=xs2[:], in_=inv2[:]),
                  reads=[r_iv2], writes=[r_xs2])
            for hc in range(NH1):
                em.op("dve", lambda e, hc=hc: e.tensor_tensor(
                    out=uscr[:, 0:512], in0=h_sb[:, hc * 512:(hc + 1) * 512],
                    in1=xs2[:], op=ALU.mult), reads=[r_h[hc], r_xs2], writes=[r_us])
                em.op("dve", lambda e: e.tensor_copy(out=qi2[:], in_=uscr[:, 0:512]),
                      reads=[r_us], writes=[r_qi2])
                em.op("dve", lambda e, hc=hc: e.tensor_copy(
                    out=h_sb[:, hc * 512:(hc + 1) * 512], in_=qi2[:]),
                    reads=[r_qi2], writes=[r_h[hc]])
            for db in range(2):
                for hc in range(NH1):
                    sl = hc % 4
                    em.dma(("sp", "q0" if hc % 2 == 0 else "q1"),
                           w2s[:, sl * D: sl * D + 512],
                           dr["W2q"].ap()[hc * 128:(hc + 1) * 128,
                                          db * 512:(db + 1) * 512],
                           writes=[r_w2s[sl]])
                    for st in range(4):
                        em.op("pe", lambda e, hc=hc, st=st, sl=sl: e.matmul(
                            ps_o[st][:, 0:512],
                            lhsT=h_sb[:, hc * 512 + st * 128: hc * 512 + (st + 1) * 128],
                            rhs=w2s[:, sl * D: sl * D + 512],
                            start=(hc == 0), stop=(hc == NH1 - 1)),
                            reads=[r_h[hc], r_w2s[sl]], writes=[r_po[st]])
                for st in range(4):
                    qt = sb * 4 + st
                    em.op("dve", lambda e, qt=qt, st=st, db=db: e.scalar_tensor_tensor(
                        out=uscr[:, 0:512], in0=ps_o[st][:, 0:512],
                        scalar=sc2[:, st:st + 1],
                        in1=b2r[:, db * 512:(db + 1) * 512],
                        op0=ALU.mult, op1=ALU.add),
                        reads=[r_po[st], r_sc2, r_b2], writes=[r_us])
                    em.op("dve", lambda e, qt=qt, db=db: e.tensor_tensor(
                        out=x_tm[:, qt * D + db * 512: qt * D + db * 512 + 512],
                        in0=x_tm[:, qt * D + db * 512: qt * D + db * 512 + 512],
                        in1=uscr[:, 0:512], op=ALU.add),
                        reads=[r_us, rx[qt]], writes=[rx[qt]])
        em.bar()



_LAYERS_ONLY = None
_SPLIT = False


def kernel(L0_attn_x, L0_attn_W, L0_attn_b, L0_ffn_w_gate, L0_ffn_w_up, L0_ffn_w_down,
           L1_attn_Wqkv, L1_attn_bqkv, L1_attn_Wout, L1_attn_bout,
           L1_ffn_W1, L1_ffn_b1, L1_ffn_W2, L1_ffn_b2):
    x = np.ascontiguousarray(np.asarray(L0_attn_x, dtype=np.float32))
    B = x.shape[0]
    w = {
        "W0": np.ascontiguousarray(np.asarray(L0_attn_W, np.float32)),
        "b0": np.ascontiguousarray(np.asarray(L0_attn_b, np.float32)),
        "wg": np.ascontiguousarray(np.asarray(L0_ffn_w_gate, np.float32)),
        "wu": np.ascontiguousarray(np.asarray(L0_ffn_w_up, np.float32)),
        "wd": np.ascontiguousarray(np.asarray(L0_ffn_w_down, np.float32)),
        "Wqkv": np.ascontiguousarray(np.asarray(L1_attn_Wqkv, np.float32)),
        "bqkv": np.ascontiguousarray(np.asarray(L1_attn_bqkv, np.float32)),
        "Wout": np.ascontiguousarray(np.asarray(L1_attn_Wout, np.float32)),
        "bout": np.ascontiguousarray(np.asarray(L1_attn_bout, np.float32)),
        "W1T": np.ascontiguousarray(np.asarray(L1_ffn_W1, np.float32).T),
        "b1": np.ascontiguousarray(np.asarray(L1_ffn_b1, np.float32)),
        "W2T": np.ascontiguousarray(np.asarray(L1_ffn_W2, np.float32).T),
        "b2": np.ascontiguousarray(np.asarray(L1_ffn_b2, np.float32)),
    }
    l0_names = ["W0", "b0", "wg", "wu", "wd"]
    l1_names = ["Wqkv", "bqkv", "Wout", "bout", "W1T", "b1", "W2T", "b2"]
    if _SPLIT:
        nc0 = build_program((0,))
        maps = [dict({"x_in": np.ascontiguousarray(x[c])}, **{n: w[n] for n in l0_names})
                for c in range(B)]
        res0 = run_bass_kernel_spmd(nc0, maps, core_ids=list(range(B)))
        x1 = np.stack([np.asarray(res0.results[c]["x_out"]) for c in range(B)])
        if _LAYERS_ONLY == (0,):
            return x1
        nc1 = build_program((1,))
        maps = [dict({"x_in": np.ascontiguousarray(x1[c])}, **{n: w[n] for n in l1_names})
                for c in range(B)]
        res1 = run_bass_kernel_spmd(nc1, maps, core_ids=list(range(B)))
        return np.stack([np.asarray(res1.results[c]["x_out"]) for c in range(B)])
    nc01 = build_program((0, 1))
    maps = [dict({"x_in": np.ascontiguousarray(x[c])},
                 **{n: w[n] for n in l0_names + l1_names}) for c in range(B)]
    res = run_bass_kernel_spmd(nc01, maps, core_ids=list(range(B)))
    return np.stack([np.asarray(res.results[c]["x_out"]) for c in range(B)])


# version 19 (final)
# speedup vs baseline: 1.0119x; 1.0119x over previous
import contextlib
import numpy as np
import ml_dtypes
import concourse.bass as bass
import concourse.mybir as mybir
from concourse.bass import AP
from concourse.bass_utils import run_bass_kernel_spmd
from bass_rust import ReduceOp


F32 = mybir.dt.float32
BF16 = mybir.dt.bfloat16
I32 = mybir.dt.int32
AF = mybir.ActivationFunctionType
ALU = mybir.AluOpType
AX = mybir.AxisListType

S = 2048
D = 1024
NT = S // 128
ND = D // 128
I0 = 2816
NI0 = I0 // 128
H1 = 4096
NH1 = H1 // 128
NHEAD = 16
HDIM = 64
EPS_LN = 1e-5
SOFTMAX_SHIFT = 16.0

ENG2BLOCK = {"sp": "sync", "act": "scalar", "dve": "vector", "pool": "gpsimd", "pe": "tensor"}
PHASE_LIMIT = 99
DBG_NODOWN = False
DBG_YSINGLE = False
DBG_DBANK1 = False
DBG_WQ = None


class Res:
    __slots__ = ("w", "readers")

    def __init__(self):
        self.w = None
        self.readers = []


class Emitter:
    ENGS = ("sp", "act", "pool", "dve", "pe")
    CHAN_GROUPS = {"q0": 6, "q1": 6, "q2": 2}
    CHANS = tuple(f"{g}_{i}" for g, n in CHAN_GROUPS.items() for i in range(n))

    def __init__(self, nc):
        self.nc = nc
        self.cnt = {n: 0 for n in self.ENGS + self.CHANS}
        self.ops = {e: [] for e in self.ENGS}
        self.last_waited = {e: {} for e in self.ENGS}
        self._bar_pending = {e: [] for e in self.ENGS}
        self.chan_uses = {g: 0 for g in self.CHAN_GROUPS}
        self.sems = {}

    def bar(self):
        snap = [(n, self.cnt[n]) for n in self.ENGS + self.CHANS if self.cnt[n] > 0]
        for e in self.ENGS:
            self._bar_pending[e] = [t for t in snap if t[0] != e]

    def op(self, eng, fn, reads=(), writes=(), extra=(), chan=None):
        waits = list(self._bar_pending[eng]) + list(extra)
        self._bar_pending[eng] = []
        for r in reads:
            if r.w is not None:
                waits.append(r.w)
        for r in writes:
            if r.w is not None:
                waits.append(r.w)
            waits.extend(r.readers)
        sem = chan if chan is not None else eng
        amt = 16 if chan is not None else 1
        eff = []
        lw = self.last_waited[eng]
        best = {}
        for s, v in waits:
            if v <= 0:
                continue
            if eng == "pe" and s == "pe":
                continue
            if v > best.get(s, -1):
                best[s] = v
        for s, v in best.items():
            if lw.get(s, -1) >= v:
                continue
            lw[s] = v
            eff.append((s, v))
        self.cnt[sem] += amt
        tok = (sem, self.cnt[sem])
        self.ops[eng].append((eff, fn, sem, amt))
        for r in reads:
            r.readers.append(tok)
        for r in writes:
            r.w = tok
            r.readers = []
        return tok

    def dma(self, eng_chan, out, in_, reads=(), writes=(), extra=()):
        eng, group = eng_chan
        n = self.CHAN_GROUPS[group]
        sem = f"{group}_{self.chan_uses[group] % n}"
        self.chan_uses[group] += 1
        prev = self.cnt[sem]
        extra = list(extra)
        if prev > 0:
            extra.append((sem, prev))
        return self.op(eng, lambda e: e.dma_start(out=out, in_=in_),
                       reads=reads, writes=writes, extra=extra, chan=sem)

    def raw_wait(self, eng, extra):
        waits = list(self._bar_pending[eng]) + list(extra)
        self._bar_pending[eng] = []
        eff = []
        lw = self.last_waited[eng]
        for s, v in waits:
            if v <= 0 or lw.get(s, -1) >= v:
                continue
            lw[s] = v
            eff.append((s, v))
        if eff:
            self.ops[eng].append((eff, None, None, 0))

    def tok(self, sem):
        return (sem, self.cnt[sem])

    def finalize(self, es):
        for n in self.ENGS + self.CHANS:
            self.sems[n] = es.enter_context(self.nc.semaphore("sem_" + n))
        block = es.enter_context(self.nc.Block())
        for eng in self.ENGS:
            entries = self.ops[eng]
            if not entries:
                continue
            section = getattr(block, ENG2BLOCK[eng])

            def make(entries):
                def _(e):
                    for waits, fn, sem, amt in entries:
                        for s, v in waits:
                            e.wait_ge(self.sems[s], v)
                        if fn is not None:
                            fn(e).then_inc(self.sems[sem], amt)
                return _
            section(make(entries))



def emit_ln_tile(em, x_tile, xn_bf_out, mean_col, rstd_col, scr, rx, rxn, rstats):
    sq, tmp, rsq, rtmp = scr["sq"], scr["tmp"], scr["r_sq"], scr["r_tmp"]
    em.op("dve", lambda e: e.tensor_reduce(out=mean_col, in_=x_tile, axis=AX.X, op=ALU.add),
          reads=[rx], writes=[rstats])
    em.op("dve", lambda e: e.tensor_scalar(out=mean_col, in0=mean_col, scalar1=1.0 / D,
                                           scalar2=None, op0=ALU.mult, op1=ALU.bypass),
          reads=[rstats], writes=[rstats])
    em.op("act", lambda e: e.activation(out=sq, in_=x_tile, func=AF.Square),
          reads=[rx], writes=[rsq])
    em.op("dve", lambda e: e.tensor_reduce(out=tmp[:, 0:1], in_=sq, axis=AX.X, op=ALU.add),
          reads=[rsq], writes=[rtmp])
    em.op("dve", lambda e: e.tensor_scalar(out=tmp[:, 0:1], in0=tmp[:, 0:1], scalar1=1.0 / D,
                                           scalar2=None, op0=ALU.mult, op1=ALU.bypass),
          reads=[rtmp], writes=[rtmp])
    em.op("dve", lambda e: e.tensor_scalar(out=tmp[:, 1:2], in0=mean_col, scalar1=mean_col,
                                           scalar2=None, op0=ALU.mult, op1=ALU.bypass),
          reads=[rstats], writes=[rtmp])
    em.op("dve", lambda e: e.scalar_tensor_tensor(out=tmp[:, 0:1], in0=tmp[:, 0:1],
                                                  scalar=EPS_LN, in1=tmp[:, 1:2],
                                                  op0=ALU.add, op1=ALU.subtract),
          reads=[rtmp], writes=[rtmp])
    em.op("act", lambda e: e.activation(out=tmp[:, 0:1], in_=tmp[:, 0:1], func=AF.Ln),
          reads=[rtmp], writes=[rtmp])
    em.op("act", lambda e: e.activation(out=rstd_col, in_=tmp[:, 0:1], func=AF.Exp,
                                        scale=-0.5), reads=[rtmp], writes=[rstats])
    t = em.op("dve", lambda e: e.tensor_scalar(out=xn_bf_out, in0=x_tile, scalar1=mean_col,
                                               scalar2=rstd_col, op0=ALU.subtract,
                                               op1=ALU.mult),
              reads=[rx, rstats], writes=[rxn])
    return t


def emit_fp4_tile(em, xin_f32, qout_bf, f4, r_in, r_out, Wd, pool_chains=False):
    nb = Wd // 16
    A = lambda k: f4[k][0]
    R = lambda k: f4[k][1]
    em.op("act", lambda e: e.activation(out=A("a")[:, 0:Wd], in_=xin_f32, func=AF.Abs),
          reads=[r_in], writes=[R("a")])
    em.op("act", lambda e: e.activation(out=A("sg")[:, 0:Wd], in_=xin_f32, func=AF.Sign),
          reads=[r_in], writes=[R("sg")])
    em.op("dve", lambda e: e.tensor_reduce(
        out=A("bmax")[:, 0:nb].rearrange("p (w o) -> p w o", o=1),
        in_=A("a")[:, 0:Wd].rearrange("p (w s) -> p w s", s=16), axis=AX.X, op=ALU.max),
        reads=[R("a")], writes=[R("bmax")])
    em.op("dve", lambda e: e.tensor_scalar(out=A("bmax")[:, 0:nb], in0=A("bmax")[:, 0:nb],
                                           scalar1=1.0 / 6.0, scalar2=1e-12,
                                           op0=ALU.mult, op1=ALU.max),
          reads=[R("bmax")], writes=[R("bmax")])
    em.op("dve", lambda e: e.tensor_scalar(out=A("hsc")[:, 0:nb], in0=A("bmax")[:, 0:nb],
                                           scalar1=0.5, scalar2=None, op0=ALU.mult,
                                           op1=ALU.bypass),
          reads=[R("bmax")], writes=[R("hsc")])
    em.op("dve", lambda e: e.reciprocal(out=A("inv")[:, 0:nb], in_=A("bmax")[:, 0:nb]),
          reads=[R("bmax")], writes=[R("inv")])
    em.op("dve", lambda e: e.tensor_scalar(out=A("inv")[:, 0:nb], in0=A("inv")[:, 0:nb],
                                           scalar1=4.0, scalar2=None, op0=ALU.mult,
                                           op1=ALU.bypass),
          reads=[R("inv")], writes=[R("inv")])
    bc = lambda t: t[:, 0:nb].rearrange("p (w o) -> p w o", o=1).broadcast_to([128, nb, 16])
    a3 = A("a")[:, 0:Wd].rearrange("p (w s) -> p w s", s=16)
    v3 = A("v")[:, 0:Wd].rearrange("p (w s) -> p w s", s=16)
    s13 = A("s1")[:, 0:Wd].rearrange("p (w s) -> p w s", s=16)
    s23 = A("s2")[:, 0:Wd].rearrange("p (w s) -> p w s", s=16)
    em.op("dve", lambda e: e.tensor_tensor(out=v3, in0=a3, in1=bc(A("inv")),
                                           op=ALU.mult),
          reads=[R("a"), R("inv")], writes=[R("v")])
    if pool_chains:
        em.op("pool", lambda e: e.tensor_scalar(out=A("a")[:, 0:Wd], in0=A("v")[:, 0:Wd],
                                                scalar1=20.0, scalar2=None, op0=ALU.is_gt,
                                                op1=ALU.bypass), reads=[R("v"), R("a")],
              writes=[R("a")])
        em.op("pool", lambda e: e.tensor_tensor(
            out=A("a")[:, 0:Wd], in0=A("a")[:, 0:Wd], in1=A("a")[:, 0:Wd],
            op=ALU.add), reads=[R("a")], writes=[R("a")])
        for thr in (10.0, 14.0):
            em.op("pool", lambda e, thr=thr: e.tensor_scalar(
                out=A("s2")[:, 0:Wd], in0=A("v")[:, 0:Wd], scalar1=thr, scalar2=None,
                op0=ALU.is_gt, op1=ALU.bypass), reads=[R("v")], writes=[R("s2")])
            em.op("pool", lambda e: e.tensor_tensor(
                out=A("a")[:, 0:Wd], in0=A("a")[:, 0:Wd], in1=A("s2")[:, 0:Wd],
                op=ALU.add), reads=[R("a"), R("s2")], writes=[R("a")])
        em.op("pool", lambda e: e.tensor_scalar(out=A("s1")[:, 0:Wd], in0=A("v")[:, 0:Wd],
                                                scalar1=1.0, scalar2=None, op0=ALU.is_gt,
                                                op1=ALU.bypass), reads=[R("v")], writes=[R("s1")])
        for thr in (3.0, 5.0, 7.0):
            em.op("pool", lambda e, thr=thr: e.tensor_scalar(
                out=A("s2")[:, 0:Wd], in0=A("v")[:, 0:Wd], scalar1=thr, scalar2=None,
                op0=ALU.is_gt, op1=ALU.bypass), reads=[R("v")], writes=[R("s2")])
            em.op("pool", lambda e: e.tensor_tensor(
                out=A("s1")[:, 0:Wd], in0=A("s1")[:, 0:Wd], in1=A("s2")[:, 0:Wd],
                op=ALU.add), reads=[R("s1"), R("s2")], writes=[R("s1")])
        em.op("dve", lambda e: e.scalar_tensor_tensor(
            out=A("s1")[:, 0:Wd], in0=A("a")[:, 0:Wd], scalar=2.0, in1=A("s1")[:, 0:Wd],
            op0=ALU.mult, op1=ALU.add), reads=[R("a"), R("s1")], writes=[R("s1")])
    else:
        em.op("dve", lambda e: e.tensor_scalar(out=A("s1")[:, 0:Wd], in0=A("v")[:, 0:Wd],
                                               scalar1=1.0, scalar2=None, op0=ALU.is_gt,
                                               op1=ALU.bypass), reads=[R("v")], writes=[R("s1")])
        for thr in (3.0, 5.0, 7.0):
            em.op("dve", lambda e, thr=thr: e.scalar_tensor_tensor(
                out=A("s1")[:, 0:Wd], in0=A("v")[:, 0:Wd], scalar=thr, in1=A("s1")[:, 0:Wd],
                op0=ALU.is_gt, op1=ALU.add), reads=[R("v"), R("s1")], writes=[R("s1")])
        em.op("dve", lambda e: e.tensor_scalar(out=A("s2")[:, 0:Wd], in0=A("v")[:, 0:Wd],
                                               scalar1=20.0, scalar2=2.0, op0=ALU.is_gt,
                                               op1=ALU.mult), reads=[R("v")], writes=[R("s2")])
        for thr in (10.0, 14.0):
            em.op("dve", lambda e, thr=thr: e.scalar_tensor_tensor(
                out=A("s2")[:, 0:Wd], in0=A("v")[:, 0:Wd], scalar=thr, in1=A("s2")[:, 0:Wd],
                op0=ALU.is_gt, op1=ALU.add), reads=[R("v"), R("s2")], writes=[R("s2")])
    if not pool_chains:
        em.op("dve", lambda e: e.scalar_tensor_tensor(
            out=A("s1")[:, 0:Wd], in0=A("s2")[:, 0:Wd], scalar=2.0, in1=A("s1")[:, 0:Wd],
            op0=ALU.mult, op1=ALU.add), reads=[R("s2"), R("s1")], writes=[R("s1")])
    s3g = A("sg")[:, 0:Wd].rearrange("p (w s) -> p w s", s=16)
    em.op("dve", lambda e: e.tensor_tensor(out=s23, in0=s3g, in1=bc(A("hsc")),
                                           op=ALU.mult),
          reads=[R("sg"), R("hsc"), R("s2")], writes=[R("s2")])
    t = em.op("dve", lambda e: e.tensor_tensor(out=qout_bf, in0=A("s1")[:, 0:Wd],
                                               in1=A("s2")[:, 0:Wd], op=ALU.mult),
              reads=[R("s1"), R("s2")], writes=[r_out])
    return t


def mk_f4(nc, ps, tag, width):
    Sb = lambda n, sh, dt: (ps.enter_context(nc.sbuf_tensor(tag + n, sh, dt)), Res())
    return {"a": Sb("a", [128, width], F32), "v": Sb("v", [128, width], F32),
            "s1": Sb("s1", [128, width], F32), "s2": Sb("s2", [128, width], F32),
            "sg": Sb("sg", [128, width], F32), "bmax": Sb("bm", [128, width // 16], F32),
            "inv": Sb("iv", [128, width // 16], F32), "hsc": Sb("hs", [128, width // 16], F32)}


def emit_transpose_batch4(em, consts, srcs, dsts3d, ptile, r_srcs, r_dst, rp):
    n = len(srcs)
    t = None
    for i, src in enumerate(srcs):
        t = em.op("pe", lambda e, i=i, src=src: e.transpose(
            ptile[:, i * 128:(i + 1) * 128], src, consts["idn_bf"][0][:]),
            reads=[r_srcs[i], consts["idn_bf"][1]], writes=[rp])
    t = em.op("dve", lambda e: e.tensor_copy(
        out=dsts3d, in_=ptile[:, 0:n * 128].rearrange("p (n m) -> p n m", m=128)),
        reads=[rp], writes=[r_dst])
    return t


def emit_transpose_128(em, consts, src_ap, dst_ap, ptile, r_src, r_dst, rp):
    t = em.op("pe", lambda e: e.transpose(ptile[:], src_ap, consts["idn_bf"][0][:]),
              reads=[r_src, consts["idn_bf"][1]], writes=[rp])
    t = em.op("dve", lambda e: e.tensor_copy(out=dst_ap, in_=ptile[:]),
              reads=[rp], writes=[r_dst])
    return t



def build_program(layers):
    nc = bass.Bass("TRN2", target_bir_lowering=False)
    em = Emitter(nc)
    es = contextlib.ExitStack()

    dr = {}
    dr["x_in"] = nc.dram_tensor("x_in", [S, D], F32, kind="ExternalInput")
    dr["x_out"] = nc.dram_tensor("x_out", [S, D], F32, kind="ExternalOutput")
    if 0 in layers:
        dr["W0"] = nc.dram_tensor("W0", [D, D], F32, kind="ExternalInput")
        dr["b0"] = nc.dram_tensor("b0", [D], F32, kind="ExternalInput")
        dr["wg"] = nc.dram_tensor("wg", [I0, D], F32, kind="ExternalInput")
        dr["wu"] = nc.dram_tensor("wu", [I0, D], F32, kind="ExternalInput")
        dr["wd"] = nc.dram_tensor("wd", [D, I0], F32, kind="ExternalInput")
        dr["wdTq"] = nc.dram_tensor("wdTq", [I0, D], BF16, kind="Internal")
        dr["wgTq"] = nc.dram_tensor("wgTq", [D, I0], BF16, kind="Internal")
        dr["wuTq"] = nc.dram_tensor("wuTq", [D, I0], BF16, kind="Internal")
        dr["xqq"] = nc.dram_tensor("xqq", [D, S], BF16, kind="Internal")
    if 1 in layers:
        dr["Wqkv"] = nc.dram_tensor("Wqkv", [D, 3 * D], F32, kind="ExternalInput")
        dr["bqkv"] = nc.dram_tensor("bqkv", [3 * D], F32, kind="ExternalInput")
        dr["Wout"] = nc.dram_tensor("Wout", [D, D], F32, kind="ExternalInput")
        dr["bout"] = nc.dram_tensor("bout", [D], F32, kind="ExternalInput")
        dr["W1T"] = nc.dram_tensor("W1T", [D, H1], F32, kind="ExternalInput")
        dr["b1"] = nc.dram_tensor("b1", [H1], F32, kind="ExternalInput")
        dr["W2T"] = nc.dram_tensor("W2T", [H1, D], F32, kind="ExternalInput")
        dr["b2"] = nc.dram_tensor("b2", [D], F32, kind="ExternalInput")
        dr["W1q"] = nc.dram_tensor("W1q", [NH1 * ND * 128, 128], BF16, kind="Internal")
        dr["W2q"] = nc.dram_tensor("W2q", [H1, D], BF16, kind="Internal")
        dr["xsave"] = nc.dram_tensor("xsave", [S, D], F32, kind="Internal")

    SbR = lambda *a: (es.enter_context(nc.sbuf_tensor(*a)), Res())
    x_tm = es.enter_context(nc.sbuf_tensor("x_tm", [128, NT * D], F32))
    rx = [Res() for _ in range(NT)]
    consts = {}
    consts["idn_bf"] = SbR("idn_bf", [128, 128], BF16)
    consts["idn_f32"] = SbR("idn_f32", [128, 128], F32)
    consts["ones_col_bf"] = SbR("ones_col_bf", [128, 1], BF16)
    consts["ones_row_bf"] = SbR("ones_row_bf", [1, 128], BF16)
    consts["ones_row_f32"] = SbR("ones_row_f32", [1, 128], F32)
    consts["iota_col"] = SbR("iota_col", [128, 1], F32)

    stats_m = es.enter_context(nc.sbuf_tensor("stats_m", [128, NT], F32))
    stats_r = es.enter_context(nc.sbuf_tensor("stats_r", [128, NT], F32))
    r_stats = Res()

    em.op("pool", lambda e: e.iota(consts["idn_bf"][0][:], pattern=[[-1, 128]], base=0,
                                   channel_multiplier=1, allow_small_or_imprecise_dtypes=True),
          writes=[consts["idn_bf"][1]])
    em.op("pool", lambda e: e.tensor_scalar(out=consts["idn_bf"][0][:],
                                            in0=consts["idn_bf"][0][:], scalar1=0.0,
                                            scalar2=None, op0=ALU.is_equal, op1=ALU.bypass),
          reads=[consts["idn_bf"][1]], writes=[consts["idn_bf"][1]])
    em.op("pool", lambda e: e.tensor_copy(out=consts["idn_f32"][0][:],
                                          in_=consts["idn_bf"][0][:]),
          reads=[consts["idn_bf"][1]], writes=[consts["idn_f32"][1]])
    em.op("pool", lambda e: e.memset(consts["ones_col_bf"][0][:], 1.0),
          writes=[consts["ones_col_bf"][1]])
    em.op("pool", lambda e: e.memset(consts["ones_row_bf"][0][:], 1.0),
          writes=[consts["ones_row_bf"][1]])
    em.op("pool", lambda e: e.memset(consts["ones_row_f32"][0][:], 1.0),
          writes=[consts["ones_row_f32"][1]])
    em.op("pool", lambda e: e.iota(consts["iota_col"][0][:], pattern=[[0, 1]], base=0,
                                   channel_multiplier=1, allow_small_or_imprecise_dtypes=True),
          writes=[consts["iota_col"][1]])
    em.bar()
    em.dma(("sp", "q0"), x_tm[:].rearrange("p (t d) -> p t d", t=NT),
           dr["x_in"].ap().rearrange("(t p) d -> p t d", p=128), writes=rx)
    em.bar()

    if 0 in layers:
        emit_layer0(nc, em, es, dr, consts, x_tm, rx,
                    stats_m, stats_r, r_stats, layer_x_dram=dr["x_in"])
    if 1 in layers:
        emit_layer1(nc, em, es, dr, consts, x_tm, rx, stats_m, stats_r, r_stats)

    em.bar()
    t_out = em.dma(("sp", "q0"), dr["x_out"].ap().rearrange("(t p) d -> p t d", p=128),
                   x_tm[:].rearrange("p (t d) -> p t d", t=NT), reads=rx)
    em.raw_wait("sp", [t_out])
    em.finalize(es)
    es.close()
    return nc



def emit_layer0(nc, em, es, dr, consts, x_tm, rx, stats_m, stats_r, r_stats,
                layer_x_dram):
    with contextlib.ExitStack() as ps:
        Sb = lambda *a: ps.enter_context(nc.sbuf_tensor(*a))
        Pb = lambda *a: ps.enter_context(nc.psum_tensor(*a))
        xn_tm = Sb("l0_xn_tm", [128, NT * D], BF16)
        r_xn = [Res() for _ in range(NT)]
        xn_fm = Sb("l0_xn_fm", [128, ND * S], BF16)
        r_xnf = Res()
        xtrb = Sb("l0_xtrb", [128, ND * 512], BF16)
        r_xtr = [Res()]
        r_xtr.append(r_xtr[0])
        probsT = Sb("l0_probsT", [128, NT * 512], BF16)
        _rpb = [Res() for _ in range(NT)]
        r_pb = [_rpb, _rpb]
        W_b = Sb("l0_W_b", [128, ND * D], BF16)
        r_W = Res()
        wstg = Sb("l0_wstg", [128, 2 * D], F32)
        r_ws = [Res(), Res()]
        b_col = Sb("l0_b_col", [128, ND], F32)
        r_bc = Res()
        sq = Sb("l0_sq", [128, D], F32)
        tmp1 = Sb("l0_tmp1", [128, 4], F32)
        ln_scr_rsq_holder = [Res()]
        sca = Sb("l0_sca", [128, 512], F32)
        scb = sca
        r_sca = Res()
        r_scb = r_sca
        f4 = mk_f4(nc, ps, "l0q", 512)
        qtile = Sb("l0q_qt", [128, 512], BF16)
        r_qt = Res()
        obuf = Sb("l0q_ob", [128, 512], BF16)
        r_ob = [Res()]
        r_ob.append(r_ob[0])
        xf = wstg[:, 0:D]
        r_xf = r_ws[0]
        xnf = sq
        r_xnf2 = ln_scr_rsq_holder[0]
        peps = [Pb("l0_peps0", [128, 512], BF16), Pb("l0_peps1", [128, 512], BF16)]
        r_pp = [Res(), Res()]
        ps_sc = Pb("l0_ps_sc", [128, 512], F32)
        ps_sc2 = Pb("l0_ps_sc2", [128, 512], F32)
        r_psc, r_psc2 = Res(), Res()
        ps_pv = Pb("l0_ps_pv", [128, 1024], F32)
        r_ppv = Res()
        ps_sum = Pb("l0_ps_sum", [128, 1], F32)
        r_psum = Res()
        ps_b = Pb("l0_ps_b", [128, 8], F32)
        r_psb = Res()

        lw_bf = Sb("l0_lw_bf", [128, 4224], BF16)
        r_lw = Res()
        ln_scr = {"sq": sq[:, 0:D], "tmp": tmp1, "r_sq": ln_scr_rsq_holder[0], "r_tmp": Res()}
        for sl0 in range(0, 4224, 1024):
            w0 = min(1024, 4224 - sl0)
            em.op("pool", lambda e, sl0=sl0, w0=w0: e.iota(
                wstg[:, 0:w0], pattern=[[-1, w0]], base=2048 - sl0,
                channel_multiplier=1, allow_small_or_imprecise_dtypes=True),
                writes=[r_ws[0]])
            em.op("act", lambda e, w0=w0: e.activation(out=wstg[:, 0:w0],
                                                       in_=wstg[:, 0:w0], func=AF.Abs),
                  reads=[r_ws[0]], writes=[r_ws[0]])
            em.op("dve", lambda e, sl0=sl0, w0=w0: e.tensor_scalar(
                out=lw_bf[:, sl0:sl0 + w0], in0=wstg[:, 0:w0], scalar1=0.5, scalar2=None,
                op0=ALU.is_gt, op1=ALU.bypass), reads=[r_ws[0]], writes=[r_lw])
            em.op("dve", lambda e, w0=w0: e.tensor_scalar(
                out=wstg[:, 0:w0], in0=wstg[:, 0:w0], scalar1=1e-7, scalar2=None,
                op0=ALU.add, op1=ALU.bypass), reads=[r_ws[0]], writes=[r_ws[0]])
            em.op("dve", lambda e, w0=w0: e.reciprocal(
                out=wstg[:, 1024:1024 + w0], in_=wstg[:, 0:w0]),
                reads=[r_ws[0]], writes=[r_ws[1]])
            em.op("dve", lambda e, sl0=sl0, w0=w0: e.tensor_tensor(
                out=lw_bf[:, sl0:sl0 + w0], in0=lw_bf[:, sl0:sl0 + w0],
                in1=wstg[:, 1024:1024 + w0], op=ALU.mult),
                reads=[r_ws[1], r_lw], writes=[r_lw])
        for t in range(NT):
            emit_ln_tile(em, x_tm[:, t * D:(t + 1) * D], xn_tm[:, t * D:(t + 1) * D],
                         stats_m[:, t:t + 1], stats_r[:, t:t + 1], ln_scr,
                         rx[t], r_xn[t], r_stats)
        if PHASE_LIMIT <= 2:
            return
        xnf3 = xn_fm[:].rearrange("p (c s) -> p c s", c=ND)
        for t in range(NT):
            for g in range(2):
                emit_transpose_batch4(
                    em, consts,
                    [xn_tm[:, t * D + (g * 4 + i) * 128: t * D + (g * 4 + i + 1) * 128]
                     for i in range(4)],
                    xnf3[:, g * 4: g * 4 + 4, t * 128:(t + 1) * 128],
                    peps[g], [r_xn[t]] * 4, r_xnf, r_pp[g])
        if PHASE_LIMIT <= 3:
            return
        t_bl = em.dma(("sp", "q0"), wstg[0:8, 0:128],
                      dr["b0"].ap().rearrange("(c p) -> c p", p=128), writes=[r_ws[0]])
        em.op("pe", lambda e: e.transpose(ps_b[:, 0:8], wstg[0:8, 0:128],
                                          consts["idn_f32"][0][0:8, 0:8]),
              reads=[r_ws[0], consts["idn_f32"][1]], writes=[r_psb])
        em.op("dve", lambda e: e.tensor_copy(out=b_col[:], in_=ps_b[:, 0:8]),
              reads=[r_psb], writes=[r_bc])
        for dc in range(ND):
            sl = dc % 2
            em.dma(("sp", "q0"), wstg[:, sl * D: sl * D + D],
                   dr["W0"].ap()[dc * 128:(dc + 1) * 128, :], writes=[r_ws[sl]])
            em.op("dve", lambda e, dc=dc, sl=sl: e.tensor_copy(
                out=W_b[:, dc * D:(dc + 1) * D], in_=wstg[:, sl * D: sl * D + D]),
                reads=[r_ws[sl]], writes=[r_W])
        if PHASE_LIMIT <= 4:
            return
        def emit_xtran_scores(qb):
            for dcp in range(ND):
                pp, rp = (ps_sc, r_psc) if dcp % 2 == 0 else (ps_sc2, r_psc2)
                for dc in range(ND):
                    em.op("pe", lambda e, dc=dc, dcp=dcp, qb=qb, pp=pp: e.matmul(
                        pp[:, 0:512],
                        lhsT=W_b[:, dc * D + dcp * 128: dc * D + (dcp + 1) * 128],
                        rhs=xn_fm[:, dc * S + qb * 512: dc * S + qb * 512 + 512],
                        start=(dc == 0), stop=(dc == ND - 1)),
                        reads=[r_W, r_xnf], writes=[rp])
                em.op("dve", lambda e, dcp=dcp, pp=pp, qb=qb: e.tensor_scalar(
                    out=xtrb[:, dcp * 512:(dcp + 1) * 512],
                    in0=pp[:, 0:512], scalar1=b_col[:, dcp:dcp + 1], scalar2=None,
                    op0=ALU.add, op1=ALU.bypass),
                    reads=[rp, r_bc], writes=[r_xtr[qb % 2]])
            for tc in range(NT):
                pp, rp = (ps_sc, r_psc) if tc % 2 == 0 else (ps_sc2, r_psc2)
                sc, rsc = (sca, r_sca) if tc % 2 == 0 else (scb, r_scb)
                for dc in range(ND):
                    em.op("pe", lambda e, dc=dc, tc=tc, qb=qb, pp=pp: e.matmul(
                        pp[:, 0:512],
                        lhsT=xn_fm[:, dc * S + tc * 128: dc * S + (tc + 1) * 128],
                        rhs=xtrb[:, dc * 512:(dc + 1) * 512],
                        start=(dc == 0), stop=(dc == ND - 1)),
                        reads=[r_xnf, r_xtr[qb % 2]], writes=[rp])
                c0 = 2048 - tc * 128 + qb * 512
                em.op("dve", lambda e, pp=pp, c0=c0, sc=sc: e.tensor_tensor(
                    out=sc[:, 0:512], in0=pp[:, 0:512], in1=lw_bf[:, c0:c0 + 512],
                    op=ALU.mult), reads=[rp, r_lw], writes=[rsc])
                em.op("act", lambda e, sc=sc: e.activation(out=sc[:, 0:512],
                                                           in_=sc[:, 0:512], func=AF.Tanh),
                      reads=[rsc], writes=[rsc])
                em.op("act", lambda e, sc=sc, tc=tc, qb=qb: e.activation(
                    out=probsT[:, tc * 512:(tc + 1) * 512],
                    in_=sc[:, 0:512], func=AF.Exp),
                    reads=[rsc], writes=[r_pb[qb % 2][tc]])
                if qb * 4 <= tc < qb * 4 + 4:
                    off = tc * 128 - qb * 512
                    base = tc * 512 + off
                    em.op("dve", lambda e, base=base: e.tensor_tensor(
                        out=probsT[:, base: base + 128],
                        in0=probsT[:, base: base + 128],
                        in1=consts["idn_bf"][0][:], op=ALU.subtract),
                        reads=[r_pb[qb % 2][tc], consts["idn_bf"][1]],
                        writes=[r_pb[qb % 2][tc]])

        def emit_pv(qb):
            for sub in range(4):
                qt = qb * 4 + sub
                for tc in range(NT):
                    base = tc * 512 + sub * 128
                    em.op("pe", lambda e, base=base, tc=tc: e.matmul(
                        ps_pv[:, 0:512],
                        lhsT=probsT[:, base: base + 128],
                        rhs=xn_tm[:, tc * D: tc * D + 512],
                        start=(tc == 0), stop=(tc == NT - 1)),
                        reads=[r_pb[qb % 2][tc], r_xn[tc]], writes=[r_ppv])
                    em.op("pe", lambda e, base=base, tc=tc: e.matmul(
                        ps_pv[:, 512:1024],
                        lhsT=probsT[:, base: base + 128],
                        rhs=xn_tm[:, tc * D + 512: tc * D + 1024],
                        start=(tc == 0), stop=(tc == NT - 1)),
                        reads=[r_pb[qb % 2][tc], r_xn[tc]], writes=[r_ppv])
                    em.op("pe", lambda e, base=base, tc=tc: e.matmul(
                        ps_sum[:, 0:1],
                        lhsT=probsT[:, base: base + 128],
                        rhs=consts["ones_col_bf"][0][:],
                        start=(tc == 0), stop=(tc == NT - 1)),
                        reads=[r_pb[qb % 2][tc], consts["ones_col_bf"][1]],
                        writes=[r_psum])
                r_tmp = ln_scr["r_tmp"]
                em.op("dve", lambda e: e.tensor_scalar(
                    out=tmp1[:, 0:1], in0=ps_sum[:, 0:1], scalar1=1e-7, scalar2=None,
                    op0=ALU.add, op1=ALU.bypass), reads=[r_psum], writes=[r_tmp])
                em.op("dve", lambda e: e.reciprocal(out=tmp1[:, 1:2], in_=tmp1[:, 0:1]),
                      reads=[r_tmp], writes=[r_tmp])
                for half in range(2):
                    em.op("dve", lambda e, qt=qt, half=half: e.scalar_tensor_tensor(
                        out=x_tm[:, qt * D + half * 512: qt * D + half * 512 + 512],
                        in0=ps_pv[:, half * 512:(half + 1) * 512], scalar=tmp1[:, 1:2],
                        in1=x_tm[:, qt * D + half * 512: qt * D + half * 512 + 512],
                        op0=ALU.mult, op1=ALU.add),
                        reads=[r_ppv, r_tmp, rx[qt]], writes=[rx[qt]])

        tq = peps
        r_tq = r_pp

        def quant_store_row(wdr_ap, src_f32, r_src, dst_dram, dst_base_fn, width):
            for s0 in range(0, width, 512):
                w0 = min(512, width - s0)
                emit_fp4_tile(em, src_f32[:, s0:s0 + w0], qtile[:, 0:w0], f4,
                              r_src, r_qt, w0)
                nn = w0 // 128
                g = 0
                gg = (s0 // 512) % 2
                emit_transpose_batch4(
                    em, consts,
                    [qtile[:, i * 128:(i + 1) * 128] for i in range(nn)],
                    obuf[:, 0:nn * 128].rearrange("p (n m) -> p n m", m=128),
                    tq[gg], [r_qt] * nn, r_ob[0], r_tq[gg])
                off, rstride = dst_base_fn(s0 // 128)
                em.dma(("pool", "q1"),
                       AP(dst_dram, off, [[rstride, 128], [128 * rstride, nn], [1, 128]]),
                       obuf[:, 0:nn * 128].rearrange("p (n m) -> p n m", m=128),
                       reads=[r_ob[0]])

        def prep_units():
            loads = []
            for wname, oname in (("wg", "wgTq"), ("wu", "wuTq")):
                for rc in range(NI0):
                    loads.append((wname, rc * 128, 0, D,
                                  dr[oname],
                                  lambda ci, rc=rc: (ci * 128 * I0 + rc * 128, I0)))
            for rc in range(ND):
                for cs in range(3):
                    c0 = cs * 1024
                    cw = min(1024, I0 - c0)
                    loads.append(("wd", rc * 128, c0, cw, dr["wdTq"],
                                  lambda ci, rc=rc, c0=c0: ((c0 // 128 + ci) * 128 * D + rc * 128, D)))

            def issue_load(k):
                wname, r0, c0, cw, _, _ = loads[k]
                sl = k % 2
                em.dma(("pool", "q1"), wstg[:, D + sl * D: D + sl * D + cw] if False else
                       wstg[:, sl * D: sl * D + cw],
                       dr[wname].ap()[r0:r0 + 128, c0:c0 + cw], writes=[r_ws[sl]])

            issue_load(0)
            for k in range(len(loads)):
                if k + 1 < len(loads):
                    issue_load(k + 1)
                _, _, _, cw, dst, base_fn = loads[k]
                sl = k % 2
                quant_store_row(None, wstg[:, sl * D: sl * D + cw], r_ws[sl], dst,
                                base_fn, cw)
                yield
            for t in range(NT):
                em.dma(("pool", "q1"), xf,
                       layer_x_dram.ap()[t * 128:(t + 1) * 128, :], writes=[r_xf])
                em.op("dve", lambda e, t=t: e.tensor_scalar(
                    out=xnf[:, 0:D], in0=xf,
                    scalar1=stats_m[:, t:t + 1], scalar2=stats_r[:, t:t + 1],
                    op0=ALU.subtract, op1=ALU.mult),
                    reads=[r_xf, r_stats], writes=[r_xnf2])
                quant_store_row(None, xnf[:, 0:D], r_xnf2, dr["xqq"],
                                lambda ci, t=t: (ci * 128 * S + t * 128, S), D)
                yield

        gen = prep_units()

        def step_gen(n=1):
            for _ in range(n):
                if next(gen, "DONE") == "DONE":
                    return

        for qb in range(4):
            emit_xtran_scores(qb)
            step_gen(5)
            emit_pv(qb)
            step_gen(13)
        step_gen(100)
        em.bar()
    if PHASE_LIMIT <= 5:
        return
    emit_layer0_ffn(nc, em, es, dr, consts, x_tm, rx, stats_m, stats_r, r_stats,
                    layer_x_dram)


def emit_layer0_ffn(nc, em, es, dr, consts, x_tm, rx, stats_m, stats_r, r_stats,
                    layer_x_dram):
    with contextlib.ExitStack() as ps:
        Sb = lambda *a: ps.enter_context(nc.sbuf_tensor(*a))
        Pb = lambda *a: ps.enter_context(nc.psum_tensor(*a))
        xq_fm = Sb("f0_xq_fm", [128, ND * S], BF16)
        r_xq = Res()
        HW = 1408
        NIH = HW // 128
        wgT = Sb("f0_wgT", [128, ND * HW], BF16)
        wuT = Sb("f0_wuT", [128, ND * HW], BF16)
        r_wg, r_wu = Res(), Res()
        hq = Sb("f0_hq", [128, 2 * NIH * 256], BF16)
        r_hq = [[[Res() for _ in range(2)] for _ in range(NIH)] for _ in range(2)]
        wdres = Sb("f0_wdres", [128, NIH * 1024], BF16)
        r_wd = [Res() for _ in range(NIH)]
        f4 = mk_f4(nc, ps, "f0", 512)
        qtile = Sb("f0_qt", [128, 2 * 512], BF16)
        r_qts = [Res(), Res()]
        hsc_f = Sb("f0_hscf", [128, 512], F32)
        hf = Sb("f0_hf", [128, 512], F32)
        r_hs, r_hf = Res(), Res()
        peps = [Pb("f0_peps0", [128, 512], BF16), Pb("f0_peps1", [128, 512], BF16)]
        r_pp = [Res(), Res()]
        ps_g = [Pb("f0_ps_g0", [128, 512], F32), Pb("f0_ps_g1", [128, 512], F32)]
        r_pg = [Res(), Res()]
        ps_y = [Pb("f0_ps_y0", [128, 512], F32), Pb("f0_ps_y1", [128, 512], F32)]
        r_py = [Res(), Res()]
        ps_d = [Pb("f0_ps_d0", [128, 512], F32), Pb("f0_ps_d1", [128, 512], F32)]
        r_pd = [Res() for _ in range(2)]

        if PHASE_LIMIT <= 6:
            return
        em.bar()
        for dc in range(ND):
            em.dma(("sp", "q0" if dc % 2 == 0 else "q2"),
                   xq_fm[:, dc * S:(dc + 1) * S],
                   dr["xqq"].ap()[dc * 128:(dc + 1) * 128, :], writes=[r_xq])
        if PHASE_LIMIT <= 7:
            return
        em.bar()
        iblocks = [(0, 512), (512, 512), (1024, 384)]
        hq3 = hq[:].rearrange("p (b c s) -> p b c s", b=2, c=NIH)

        def emit_gateyh(half, ssb):
            buf = ssb % 2
            pend = []
            qct = [0]

            def flush_pend():
                while pend:
                    pend.pop(0)()
            for st in range(2):
                qt = ssb * 2 + st
                for ibi, (i0b, iw) in enumerate(iblocks):
                    pg, rpg = ps_g[ibi % 2], r_pg[ibi % 2]
                    for dc in range(ND):
                        em.op("pe", lambda e, dc=dc, qt=qt, i0b=i0b, iw=iw, pg=pg: e.matmul(
                            pg[:, 0:iw],
                            lhsT=xq_fm[:, dc * S + qt * 128: dc * S + (qt + 1) * 128],
                            rhs=wgT[:, dc * HW + i0b: dc * HW + i0b + iw],
                            start=(dc == 0), stop=(dc == ND - 1)),
                            reads=[r_xq, r_wg], writes=[rpg])
                    py, rpy = ((ps_y[0], r_py[0]) if DBG_YSINGLE else (ps_y[ibi % 2], r_py[ibi % 2]))
                    for dc in range(ND):
                        em.op("pe", lambda e, dc=dc, qt=qt, i0b=i0b, iw=iw, py=py: e.matmul(
                            py[:, 0:iw],
                            lhsT=xq_fm[:, dc * S + qt * 128: dc * S + (qt + 1) * 128],
                            rhs=wuT[:, dc * HW + i0b: dc * HW + i0b + iw],
                            start=(dc == 0), stop=(dc == ND - 1)),
                            reads=[r_xq, r_wu], writes=[rpy])
                    em.op("act", lambda e, iw=iw, pg=pg: e.activation(
                        out=hsc_f[:, 0:iw], in_=pg[:, 0:iw], func=AF.Silu),
                        reads=[rpg], writes=[r_hs])
                    em.op("dve", lambda e, iw=iw, py=py: e.tensor_tensor(
                        out=hf[:, 0:iw], in0=hsc_f[:, 0:iw], in1=py[:, 0:iw],
                        op=ALU.mult), reads=[r_hs, rpy], writes=[r_hf])
                    qslot = qct[0] % 2
                    qct[0] += 1
                    flush_pend()
                    emit_fp4_tile(em, hf[:, 0:iw], qtile[:, qslot * 512: qslot * 512 + iw],
                                  f4, r_hf, r_qts[qslot], iw)

                    def mk_tp(buf=buf, st=st, i0b=i0b, iw=iw, qslot=qslot):
                        def go():
                            nn = iw // 128
                            for g in range((nn + 3) // 4):
                                cnt = min(4, nn - g * 4)
                                ic0 = i0b // 128 + g * 4
                                emit_transpose_batch4(
                                    em, consts,
                                    [qtile[:, qslot * 512 + (g * 4 + i) * 128:
                                           qslot * 512 + (g * 4 + i + 1) * 128]
                                     for i in range(cnt)],
                                    hq3[:, buf, ic0:ic0 + cnt, st * 128:(st + 1) * 128],
                                    peps[g % 2], [r_qts[qslot]] * cnt,
                                    r_hq[buf][ic0][st], r_pp[g % 2])
                                for ii in range(1, cnt):
                                    r_hq[buf][ic0 + ii][st].w = r_hq[buf][ic0][st].w
                                    r_hq[buf][ic0 + ii][st].readers = r_hq[buf][ic0][st].readers
                        return go
                    pend.append(mk_tp())
            flush_pend()

        def emit_down(half, ssb):
            buf = ssb % 2
            for db in range(2):
                for st in range(2):
                    bk = 0 if DBG_DBANK1 else (db * 2 + st) % 2
                    for ici in range(NIH):
                        em.op("pe", lambda e, ici=ici, st=st, db=db,
                              buf=buf, bk=bk: e.matmul(
                            ps_d[bk][:, 0:512],
                            lhsT=hq3[:, buf, ici, st * 128:(st + 1) * 128],
                            rhs=wdres[:, ici * 1024 + db * 512: ici * 1024 + db * 512 + 512],
                            start=(ici == 0), stop=(ici == NIH - 1)),
                            reads=[r_hq[buf][ici][st], r_wd[ici]],
                            writes=[r_pd[bk]])
                    qt = ssb * 2 + st
                    em.op("dve", lambda e, qt=qt, db=db, bk=bk: e.tensor_tensor(
                        out=x_tm[:, qt * D + db * 512: qt * D + db * 512 + 512],
                        in0=x_tm[:, qt * D + db * 512: qt * D + db * 512 + 512],
                        in1=ps_d[bk][:, 0:512],
                        op=ALU.add),
                        reads=[r_pd[bk], rx[qt]], writes=[rx[qt]])

        for half in range(2):
            h0 = half * HW
            for dc in range(ND):
                em.dma(("sp", "q1"), wgT[:, dc * HW:(dc + 1) * HW],
                       dr["wgTq"].ap()[dc * 128:(dc + 1) * 128, h0:h0 + HW],
                       writes=[r_wg])
                em.dma(("sp", "q0"), wuT[:, dc * HW:(dc + 1) * HW],
                       dr["wuTq"].ap()[dc * 128:(dc + 1) * 128, h0:h0 + HW],
                       writes=[r_wu])
            for ici in range(NIH):
                em.dma(("pool", "q1" if ici % 2 == 0 else "q2"),
                       wdres[:, ici * 1024:(ici + 1) * 1024],
                       dr["wdTq"].ap()[(half * NIH + ici) * 128:
                                       (half * NIH + ici + 1) * 128, :],
                       writes=[r_wd[ici]])
            for ssb in range(8):
                emit_gateyh(half, ssb)
                if ssb > 0 and not DBG_NODOWN:
                    emit_down(half, ssb - 1)
            if not DBG_NODOWN:
                emit_down(half, 7)
        em.bar()


ALIBI_SLOPES = [2.0 ** (-0.5 * i) for i in range(NHEAD)]


def emit_layer1(nc, em, es, dr, consts, x_tm, rx, stats_m, stats_r, r_stats):
    with contextlib.ExitStack() as outer:
        aofm = outer.enter_context(nc.sbuf_tensor("l1_aofm", [128, ND * S], BF16))
        r_aofm = [Res() for _ in range(ND)]
        emit_layer1_attn(nc, em, outer, dr, consts, x_tm, rx, stats_m, stats_r, r_stats,
                         aofm, r_aofm)
        emit_layer1_wout(nc, em, outer, dr, consts, x_tm, rx, aofm, r_aofm)
    if PHASE_LIMIT <= 12:
        return
    emit_layer1_ffn(nc, em, es, dr, consts, x_tm, rx, stats_m, stats_r, r_stats)


def emit_layer1_attn(nc, em, es, dr, consts, x_tm, rx, stats_m, stats_r, r_stats,
                     aofm, r_aofm):
    with contextlib.ExitStack() as ps:
        Sb = lambda *a: ps.enter_context(nc.sbuf_tensor(*a))
        Pb = lambda *a: ps.enter_context(nc.psum_tensor(*a))
        xn_fm = Sb("l1_xn_fm", [128, ND * S], BF16)
        r_xnf = Res()
        xn_t = Sb("l1_xn_t", [128, D], BF16)
        r_xnt = [Res()]
        sq = Sb("l1_sq", [128, D], F32)
        tmp1 = Sb("l1_tmp1", [128, 4], F32)
        wqs = Sb("l1_wqs", [128, 2 * 128], F32)
        r_wqs = [Res(), Res()]
        wq_b = Sb("l1_wq_b", [128, 2 * 3 * ND * 128], BF16)
        r_wqb = [Res(), Res()]
        qkv_fm = Sb("l1_qkv_fm", [128, 3 * S], BF16)
        r_qkv = Res()
        Qp = Sb("l1_Qp", [68, S], BF16)
        Qm = Sb("l1_Qm", [68, S], BF16)
        Kx = Sb("l1_Kx", [68, S], BF16)
        r_Qp, r_Qm, r_Kx = Res(), Res(), Res()
        V1 = Sb("l1_V1", [128, NT * 65], BF16)
        r_V1 = Res()
        probsT = Sb("l1_probsT", [128, NT * 512], BF16)
        r_pb = [Res() for _ in range(NT)]
        wh = Sb("l1_wh", [128, 896], BF16)
        bandc = Sb("l1_bandc", [128, 896], F32)
        r_wh, r_bandc = Res(), Res()
        biasm = Sb("l1_biasm", [128, 1], F32)
        r_bm = Res()
        iota_row = Sb("l1_iotar", [1, S], F32)
        r_ior = Res()

        srow = sq[0:1, 0:512]
        rrep = sq[0:64, 512:1024]
        r_srow = Res()
        r_rrep = Res()
        exps = xn_t[:, 0:512]
        rampB = Sb("l1_rampB", [1, S // 2], BF16)
        r_rsB = Res()
        r_exps = [Res()]
        r_exps.append(r_exps[0])
        bq_col = Sb("l1_bq_col", [128, 24], F32)
        r_bqc = Res()
        peps = [Pb("l1_peps0", [128, 512], BF16)]
        peps.append(peps[0])
        r_pp = [Res()]
        r_pp.append(r_pp[0])
        ps_a = Pb("l1_ps_a", [128, 1024], F32)
        ps_b2 = Pb("l1_ps_b2", [128, 1024], F32)
        r_psa, r_psb = Res(), Res()
        ps_pv = [Pb("l1_ps_pv0", [128, 512], F32), Pb("l1_ps_pv1", [128, 512], F32)]
        ps_rep = Pb("l1_ps_rep", [64, 512], F32)
        r_prep = Res()
        r_ppv = [Res(), Res()]
        r_tmp = Res()

        em.bar()
        ln_scr = {"sq": sq[:, 0:D], "tmp": tmp1, "r_sq": Res(), "r_tmp": r_tmp}
        em.dma(("sp", "q0"), dr["xsave"].ap().rearrange("(t p) d -> p t d", p=128),
               x_tm[:].rearrange("p (t d) -> p t d", t=NT), reads=rx)
        for t in range(NT):
            emit_ln_tile(em, x_tm[:, t * D:(t + 1) * D], xn_t[:, 0:D],
                         stats_m[:, t:t + 1], stats_r[:, t:t + 1], ln_scr,
                         rx[t], r_xnt[0], r_stats)
            xnf3 = xn_fm[:].rearrange("p (c s) -> p c s", c=ND)
            for g in range(2):
                emit_transpose_batch4(
                    em, consts,
                    [xn_t[:, (g * 4 + i) * 128:(g * 4 + i + 1) * 128] for i in range(4)],
                    xnf3[:, g * 4: g * 4 + 4, t * 128:(t + 1) * 128],
                    peps[g], [r_xnt[0]] * 4, r_xnf, r_pp[g])
        em.op("pool", lambda e: e.iota(iota_row[:], pattern=[[1, S]], base=0,
                                       channel_multiplier=0,
                                       allow_small_or_imprecise_dtypes=True),
              writes=[r_ior])
        em.op("pool", lambda e: e.memset(biasm[:, 0:1], -SOFTMAX_SHIFT), writes=[r_bm])
        em.op("pool", lambda e: e.iota(bandc[:], pattern=[[-1, 896]], base=384,
                                       channel_multiplier=1,
                                       allow_small_or_imprecise_dtypes=True),
              writes=[r_bandc])
        em.op("act", lambda e: e.activation(out=bandc[:], in_=bandc[:], func=AF.Abs),
              reads=[r_bandc], writes=[r_bandc])
        em.dma(("sp", "q0"), sq[0:24, 0:128],
               dr["bqkv"].ap().rearrange("(c p) -> c p", p=128), writes=[ln_scr["r_sq"]])
        em.op("pe", lambda e: e.transpose(ps_a[:, 0:24], sq[0:24, 0:128],
                                          consts["idn_f32"][0][0:24, 0:24]),
              reads=[ln_scr["r_sq"], consts["idn_f32"][1]], writes=[r_psa])
        em.op("dve", lambda e: e.tensor_copy(out=bq_col[:], in_=ps_a[:, 0:24]),
              reads=[r_psa], writes=[r_bqc])
        em.bar()

        def load_wqb(pair, B):
            for kind in range(3):
                for dc in range(ND):
                    sl = dc % 2
                    m0k = kind * D + pair * 128
                    em.dma(("sp", "q0"), wqs[:, sl * 128: sl * 128 + 128],
                           dr["Wqkv"].ap()[dc * 128:(dc + 1) * 128, m0k:m0k + 128],
                           writes=[r_wqs[sl]])
                    em.op("dve", lambda e, kind=kind, dc=dc, sl=sl, B=B: e.tensor_copy(
                        out=wq_b[:, (B * 3 * ND + kind * ND + dc) * 128:
                                 (B * 3 * ND + kind * ND + dc + 1) * 128],
                        in_=wqs[:, sl * 128: sl * 128 + 128]),
                        reads=[r_wqs[sl]], writes=[r_wqb[B]])

        load_wqb(0, 0)
        for pair in range(8):
            B = pair % 2
            for kind in range(3):
                for sb in range(4):
                    pp, rp = (ps_a, r_psa) if sb % 2 == 0 else (ps_b2, r_psb)
                    for dc in range(ND):
                        em.op("pe", lambda e, kind=kind, dc=dc, sb=sb, pp=pp,
                              B=B: e.matmul(
                            pp[:, 0:512],
                            lhsT=wq_b[:, (B * 3 * ND + kind * ND + dc) * 128:
                                      (B * 3 * ND + kind * ND + dc + 1) * 128],
                            rhs=xn_fm[:, dc * S + sb * 512: dc * S + sb * 512 + 512],
                            start=(dc == 0), stop=(dc == ND - 1)),
                            reads=[r_wqb[B], r_xnf], writes=[rp])
                    em.op("act", lambda e, kind=kind, sb=sb, pp=pp, pair=pair: e.activation(
                        out=qkv_fm[:, kind * S + sb * 512: kind * S + sb * 512 + 512],
                        in_=pp[:, 0:512], func=AF.Identity,
                        bias=bq_col[:, kind * 8 + pair: kind * 8 + pair + 1]),
                        reads=[rp, r_bqc], writes=[r_qkv])
            if pair + 1 < 8:
                load_wqb(pair + 1, (pair + 1) % 2)
            for half in range(2):
                h = 2 * pair + half
                slope = ALIBI_SLOPES[h]
                em.op("pool", lambda e: e.memset(V1[:], 1.0), writes=[r_V1])
                V13 = V1[:].rearrange("p (t c) -> p t c", c=65)
                for tg in range(4):
                    q = tg % 2
                    for i in range(4):
                        tc = tg * 4 + i
                        em.op("pe", lambda e, half=half, tc=tc, q=q, i=i: e.transpose(
                            peps[q][0:128, i * 64:(i + 1) * 64],
                            qkv_fm[64 * half:64 * half + 64,
                                   2 * S + tc * 128: 2 * S + (tc + 1) * 128],
                            consts["idn_bf"][0][64 * half:64 * half + 64,
                                                64 * half:64 * half + 64]),
                            reads=[r_qkv, consts["idn_bf"][1]], writes=[r_pp[q]])
                    em.op("dve", lambda e, tg=tg, q=q, V13=V13: e.tensor_copy(
                        out=V13[:, tg * 4: tg * 4 + 4, 0:64],
                        in_=peps[q][0:128, 0:256].rearrange("p (n m) -> p n m", m=64)),
                        reads=[r_pp[q]], writes=[r_V1])
                em.op("dve", lambda e, half=half: e.tensor_copy(
                    out=Qp[0:64, :], in_=qkv_fm[64 * half:64 * half + 64, 0:S]),
                    reads=[r_qkv], writes=[r_Qp])
                em.op("dve", lambda e, half=half: e.tensor_copy(
                    out=Qm[0:64, :], in_=qkv_fm[64 * half:64 * half + 64, 0:S]),
                    reads=[r_qkv], writes=[r_Qm])
                em.op("dve", lambda e, half=half: e.tensor_copy(
                    out=Kx[0:64, :], in_=qkv_fm[64 * half:64 * half + 64, S:2 * S]),
                    reads=[r_qkv], writes=[r_Kx])
                em.op("pool", lambda e: e.memset(Qp[64:68, :], 1.0),
                      reads=[r_Qp], writes=[r_Qp])
                for sgn, Qx, rQ in ((1.0, Qp, r_Qp), (-1.0, Qm, r_Qm)):
                    if sgn < 0:
                        em.op("pool", lambda e: e.memset(Qm[64:68, :], -1.0),
                              reads=[r_Qm], writes=[r_Qm])
                    for hh in range(2):
                        c0, c1 = hh * 1024, (hh + 1) * 1024
                        em.op("dve", lambda e, slope=slope, sgn=sgn, c0=c0, c1=c1:
                              e.tensor_scalar(
                            out=rampB[:], in0=iota_row[0:1, c0:c1],
                            scalar1=sgn * 8.0 * slope, scalar2=None,
                            op0=ALU.mult, op1=ALU.bypass),
                            reads=[r_ior], writes=[r_rsB])
                        em.dma(("sp", "q0"), Qx[64:65, c0:c1], rampB[:],
                               reads=[r_rsB], writes=[rQ])
                        em.op("dve", lambda e, slope=slope, sgn=sgn, c0=c0, c1=c1:
                              e.scalar_tensor_tensor(
                            out=rampB[:], in0=iota_row[0:1, c0:c1],
                            scalar=sgn * 8.0 * slope, in1=rampB[:],
                            op0=ALU.mult, op1=ALU.subtract),
                            reads=[r_ior], writes=[r_rsB])
                        em.dma(("sp", "q0"), Qx[65:66, c0:c1], rampB[:],
                               reads=[r_rsB], writes=[rQ])
                        if sgn < 0:
                            em.dma(("sp", "q0"), Kx[67:68, c0:c1], rampB[:],
                                   reads=[r_rsB], writes=[r_Kx])
                em.op("pool", lambda e: e.memset(Kx[64:66, :], 1.0),
                      reads=[r_Kx], writes=[r_Kx])
                em.dma(("sp", "q0"), Kx[66:67, :], Qm[64:65, :], reads=[r_Qm],
                       writes=[r_Kx])

                em.op("act", lambda e, slope=slope: e.activation(
                    out=wh[:], in_=bandc[:], func=AF.Exp, scale=-slope),
                    reads=[r_bandc], writes=[r_wh])
                def emit_norm(qb):
                    pv, rpv = ps_pv[qb % 2], r_ppv[qb % 2]
                    em.op("dve", lambda e, pv=pv: e.tensor_copy(
                        out=srow, in_=pv[64:65, 0:512]),
                        reads=[rpv, ln_scr["r_sq"]], writes=[r_srow])
                    em.op("pe", lambda e: e.matmul(
                        ps_rep[0:64, 0:512], lhsT=consts["ones_row_f32"][0][0:1, 0:64],
                        rhs=srow, start=True, stop=True),
                        reads=[r_srow, consts["ones_row_f32"][1]], writes=[r_prep])
                    em.op("dve", lambda e: e.reciprocal(
                        out=rrep, in_=ps_rep[0:64, 0:512]),
                        reads=[r_prep, ln_scr["r_sq"]], writes=[r_rrep])
                    em.op("dve", lambda e, pv=pv, h=h, qb=qb, half=half: e.tensor_tensor(
                        out=aofm[64 * half:64 * half + 64,
                                 (h // 2) * S + qb * 512:(h // 2) * S + qb * 512 + 512],
                        in0=pv[0:64, 0:512], in1=rrep, op=ALU.mult),
                        reads=[rpv, r_rrep], writes=[r_aofm[h // 2]])

                for qb in range(4):
                    lo = qb * 4
                    pairs = [(t, t + 1) for t in range(0, lo, 2)] + \
                            [(t, t + 1) for t in range(lo + 4, NT, 2)]
                    pcnt = 0
                    for (t0, t1) in pairs:
                        Qx, rQx = (Qm, r_Qm) if (t1 + 1) * 128 <= qb * 512 + 512 \
                            else (Qp, r_Qp)
                        pp, rp = (ps_a, r_psa) if pcnt % 2 == 0 else (ps_b2, r_psb)
                        pcnt += 1
                        for i, tc in enumerate((t0, t1)):
                            em.op("pe", lambda e, tc=tc, qb=qb, pp=pp, Qx=Qx,
                                  i=i: e.matmul(
                                pp[:, i * 512:(i + 1) * 512],
                                lhsT=Kx[0:68, tc * 128:(tc + 1) * 128],
                                rhs=Qx[0:68, qb * 512: qb * 512 + 512],
                                start=True, stop=True),
                                reads=[r_Kx, rQx], writes=[rp])
                        em.op("act", lambda e, t0=t0, pp=pp: e.activation(
                            out=probsT[:, t0 * 512:(t0 + 2) * 512], in_=pp[:, 0:1024],
                            func=AF.Exp, scale=0.125, bias=biasm[:, 0:1]),
                            reads=[rp, r_bm], writes=[r_pb[t0], r_pb[t1]])
                    for tc in range(lo, lo + 4):
                        pp, rp = (ps_a, r_psa) if pcnt % 2 == 0 else (ps_b2, r_psb)
                        pcnt += 1
                        em.op("pe", lambda e, tc=tc, qb=qb, pp=pp: e.matmul(
                            pp[:, 0:512],
                            lhsT=Kx[0:64, tc * 128:(tc + 1) * 128],
                            rhs=Qp[0:64, qb * 512: qb * 512 + 512],
                            start=True, stop=True),
                            reads=[r_Kx, r_Qp], writes=[rp])
                        em.op("act", lambda e, pp=pp: e.activation(
                            out=exps[:, 0:512], in_=pp[:, 0:512],
                            func=AF.Exp, scale=0.125, bias=biasm[:, 0:1]),
                            reads=[rp, r_bm], writes=[r_exps[0]])
                        c0 = 384 - tc * 128 + qb * 512
                        em.op("dve", lambda e, tc=tc, c0=c0: e.tensor_tensor(
                            out=probsT[:, tc * 512:(tc + 1) * 512],
                            in0=exps[:, 0:512],
                            in1=wh[:, c0:c0 + 512], op=ALU.mult),
                            reads=[r_exps[0], r_wh], writes=[r_pb[tc]])
                    pv, rpv = ps_pv[qb % 2], r_ppv[qb % 2]
                    for tc in range(NT):
                        em.op("pe", lambda e, tc=tc, pv=pv: e.matmul(
                            pv[0:65, 0:512],
                            lhsT=V1[:, tc * 65:(tc + 1) * 65],
                            rhs=probsT[:, tc * 512:(tc + 1) * 512],
                            start=(tc == 0), stop=(tc == NT - 1)),
                            reads=[r_pb[tc], r_V1], writes=[rpv])
                    if qb > 0:
                        emit_norm(qb - 1)
                emit_norm(3)
        em.bar()


def emit_layer1_wout(nc, em, es, dr, consts, x_tm, rx, aofm, r_aofm):
    with contextlib.ExitStack() as ps:
        Sb = lambda *a: ps.enter_context(nc.sbuf_tensor(*a))
        Pb = lambda *a: ps.enter_context(nc.psum_tensor(*a))
        wout_b = Sb("l1_wout_b", [128, ND * D], BF16)
        r_wout = Res()
        wstg = Sb("l1_wstg", [128, 2 * D], F32)
        r_ws = [Res(), Res()]
        brow = Sb("l1_brow", [1, D], BF16)
        brow_f = Sb("l1_brow_f", [1, D], F32)
        r_brow = Res()
        ps_a = Pb("l1w_ps_a", [128, 512], F32)
        ps_b2 = Pb("l1w_ps_b2", [128, 512], F32)
        r_psa, r_psb = Res(), Res()
        em.bar()
        for dc in range(ND):
            sl = dc % 2
            em.dma(("sp", "q0"), wstg[:, sl * D: sl * D + D],
                   dr["Wout"].ap()[dc * 128:(dc + 1) * 128, :], writes=[r_ws[sl]])
            em.op("dve", lambda e, dc=dc, sl=sl: e.tensor_copy(
                out=wout_b[:, dc * D:(dc + 1) * D], in_=wstg[:, sl * D: sl * D + D]),
                reads=[r_ws[sl]], writes=[r_wout])
        em.dma(("sp", "q0"), brow_f[0:1, 0:D],
               dr["bout"].ap().rearrange("(a d) -> a d", a=1), writes=[r_brow])
        em.op("dve", lambda e: e.tensor_copy(out=brow[0:1, 0:D], in_=brow_f[0:1, 0:D]),
              reads=[r_brow], writes=[r_brow])
        em.bar()
        for t in range(NT):
            for db in range(2):
                pp, rp = (ps_a, r_psa) if db == 0 else (ps_b2, r_psb)
                for dc in range(ND):
                    em.op("pe", lambda e, dc=dc, db=db, pp=pp, t=t: e.matmul(
                        pp[:, 0:512],
                        lhsT=aofm[:, dc * S + t * 128: dc * S + (t + 1) * 128],
                        rhs=wout_b[:, dc * D + db * 512: dc * D + db * 512 + 512],
                        start=(dc == 0), stop=False),
                        reads=[r_aofm[dc], r_wout], writes=[rp])
                em.op("pe", lambda e, db=db, pp=pp: e.matmul(
                    pp[:, 0:512], lhsT=consts["ones_row_bf"][0][:],
                    rhs=brow[0:1, db * 512: db * 512 + 512],
                    start=False, stop=True),
                    reads=[consts["ones_row_bf"][1], r_brow], writes=[rp])
                em.op("dve", lambda e, t=t, db=db, pp=pp: e.tensor_tensor(
                    out=x_tm[:, t * D + db * 512: t * D + db * 512 + 512],
                    in0=x_tm[:, t * D + db * 512: t * D + db * 512 + 512],
                    in1=pp[:, 0:512], op=ALU.add),
                    reads=[rp, rx[t]], writes=[rx[t]])
        em.bar()


def emit_layer1_ffn(nc, em, es, dr, consts, x_tm, rx, stats_m, stats_r, r_stats):
    with contextlib.ExitStack() as outer:
        wsv = outer.enter_context(nc.sbuf_tensor("f1_wsv", [128, 2], F32))
        r_wsv = Res()
        emit_l1ffn_prep(nc, em, outer, dr, consts, wsv, r_wsv)
        if PHASE_LIMIT <= 13:
            return
        emit_l1ffn_main(nc, em, outer, dr, consts, x_tm, rx, stats_m, stats_r, r_stats,
                        wsv, r_wsv)


def emit_l1ffn_prep(nc, em, es, dr, consts, wsv, r_wsv):
    with contextlib.ExitStack() as ps:
        Sb = lambda *a: ps.enter_context(nc.sbuf_tensor(*a))
        wstg = Sb("f1p_wstg", [128, 8 * 2048], F32)
        r_ws = [Res() for _ in range(8)]
        tscr = Sb("f1p_tscr", [128, 2048], F32)
        r_tscr = Res()
        tbf = Sb("f1p_tbf", [128, 4 * 2048], BF16)
        r_tbf = [Res() for _ in range(4)]
        acc = Sb("f1p_acc", [128, 8], F32)
        r_acc = Res()
        thr = Sb("f1p_thr", [128, 4], F32)
        r_thr = Res()
        onesf = Sb("f1p_onesf", [128, 1], F32)
        r_of = Res()
        onesr = Sb("f1p_onesr", [1, 128], F32)
        r_or = Res()
        srow = Sb("f1p_srow", [1, 1], F32)
        r_sr = Res()
        pss = ps.enter_context(nc.psum_tensor("f1p_pss", [128, 1], F32))
        r_psx = Res()
        em.bar()
        em.op("pool", lambda e: e.memset(onesf[:], 1.0), writes=[r_of])
        em.op("pool", lambda e: e.memset(onesr[:], 1.0), writes=[r_or])
        pieces = {0: [], 1: []}
        for wi, (wname, nchunk, width) in enumerate((("W1T", ND, H1), ("W2T", NH1, D))):
            npiece = max(width // 2048, 1)
            pw = min(width, 2048)
            for rc in range(nchunk):
                for pc in range(npiece):
                    pieces[wi].append((wname, rc, pc, pw))

        def pass1(wi):
            first = True
            plist = pieces[wi]

            def issue(k):
                wname, rc, pc, pw = plist[k]
                sl = k % 4
                em.dma(("sp", "q0" if k % 2 == 0 else "q1"),
                       wstg[:, sl * 2048: sl * 2048 + pw],
                       dr[wname].ap()[rc * 128:(rc + 1) * 128,
                                      pc * 2048: pc * 2048 + pw],
                       writes=[r_ws[sl]])
            for k in range(min(4, len(plist))):
                issue(k)
            for k, (wname, rc, pc, pw) in enumerate(plist):
                sl = k % 4
                em.op("dve", lambda e, sl=sl, pw=pw: e.tensor_reduce(
                    out=acc[:, 1:2], in_=wstg[:, sl * 2048: sl * 2048 + pw],
                    axis=AX.X, op=ALU.add, apply_absolute_value=True),
                    reads=[r_ws[sl]], writes=[r_acc])
                if k + 4 < len(plist):
                    issue(k + 4)
                if first:
                    em.op("dve", lambda e: e.tensor_copy(out=acc[:, 0:1], in_=acc[:, 1:2]),
                          reads=[r_acc], writes=[r_acc])
                    first = False
                else:
                    em.op("dve", lambda e: e.tensor_tensor(
                        out=acc[:, 0:1], in0=acc[:, 0:1], in1=acc[:, 1:2], op=ALU.add),
                        reads=[r_acc], writes=[r_acc])
            em.op("pe", lambda e: e.matmul(pss[0:1, 0:1], lhsT=acc[:, 0:1],
                                           rhs=onesf[:, 0:1], start=True, stop=True),
                  reads=[r_acc, r_of], writes=[r_psx])
            em.op("dve", lambda e: e.tensor_copy(out=srow[0:1, 0:1], in_=pss[0:1, 0:1]),
                  reads=[r_psx], writes=[r_sr])
            em.op("pe", lambda e: e.matmul(pss[:, 0:1], lhsT=onesr[0:1, 0:128],
                                           rhs=srow[0:1, 0:1], start=True, stop=True),
                  reads=[r_sr, r_or], writes=[r_psx])
            em.op("dve", lambda e: e.tensor_copy(out=acc[:, 2:3], in_=pss[:, 0:1]),
                  reads=[r_psx], writes=[r_acc])
            em.op("dve", lambda e, wi=wi: e.tensor_scalar(
                out=wsv[:, wi:wi + 1], in0=acc[:, 2:3], scalar1=1.0 / (D * H1),
                scalar2=1e-5, op0=ALU.mult, op1=ALU.max),
                reads=[r_acc], writes=[r_wsv])
            em.op("dve", lambda e, wi=wi: e.tensor_scalar(
                out=thr[:, 2 * wi: 2 * wi + 1], in0=wsv[:, wi:wi + 1], scalar1=0.5,
                scalar2=None, op0=ALU.mult, op1=ALU.bypass),
                reads=[r_wsv], writes=[r_thr])
            em.op("dve", lambda e, wi=wi: e.tensor_scalar(
                out=thr[:, 2 * wi + 1: 2 * wi + 2], in0=thr[:, 2 * wi: 2 * wi + 1],
                scalar1=-1.0, scalar2=None, op0=ALU.mult, op1=ALU.bypass),
                reads=[r_thr], writes=[r_thr])

        def pass2(wi):
            plist = pieces[wi]

            def issue(k):
                wname, rc, pc, pw = plist[k]
                sl = 4 + k % 4
                em.dma(("sp", "q0" if k % 2 == 0 else "q1"),
                       wstg[:, sl * 2048: sl * 2048 + pw],
                       dr[wname].ap()[rc * 128:(rc + 1) * 128,
                                      pc * 2048: pc * 2048 + pw],
                       writes=[r_ws[sl]])
            for k in range(min(4, len(plist))):
                issue(k)
            for k, (wname, rc, pc, pw) in enumerate(plist):
                sl = 4 + k % 4
                tb = k % 4
                em.op("dve", lambda e, sl=sl, pw=pw, wi=wi: e.tensor_scalar(
                    out=tscr[:, 0:pw], in0=wstg[:, sl * 2048: sl * 2048 + pw],
                    scalar1=thr[:, 2 * wi: 2 * wi + 1], scalar2=None,
                    op0=ALU.is_gt, op1=ALU.bypass),
                    reads=[r_ws[sl], r_thr], writes=[r_tscr])
                em.op("dve", lambda e, sl=sl, pw=pw, wi=wi, tb=tb: e.scalar_tensor_tensor(
                    out=tbf[:, tb * 2048: tb * 2048 + pw],
                    in0=wstg[:, sl * 2048: sl * 2048 + pw],
                    scalar=thr[:, 2 * wi + 1: 2 * wi + 2], in1=tscr[:, 0:pw],
                    op0=ALU.is_lt, op1=ALU.subtract),
                    reads=[r_ws[sl], r_thr, r_tscr], writes=[r_tbf[tb]])
                if k + 4 < len(plist):
                    issue(k + 4)
                sq_ = "q2" if k % 2 == 0 else "q1"
                if wi == 0:
                    off = ((pc * 16 * ND + rc) * 128) * 128
                    em.dma(("pool", sq_),
                           AP(dr["W1q"], off,
                              [[128, 128], [ND * 128 * 128, 16], [1, 128]]),
                           tbf[:, tb * 2048: tb * 2048 + 2048]
                           .rearrange("p (h c) -> p h c", c=128),
                           reads=[r_tbf[tb]])
                else:
                    em.dma(("pool", sq_), dr["W2q"].ap()[rc * 128:(rc + 1) * 128, :],
                           tbf[:, tb * 2048: tb * 2048 + D], reads=[r_tbf[tb]])

        pass1(0)
        pass1(1)
        pass2(0)
        pass2(1)
        em.bar()


def emit_l1ffn_main(nc, em, es, dr, consts, x_tm, rx, stats_m, stats_r, r_stats,
                    wsv, r_wsv):
    with contextlib.ExitStack() as ps:
        Sb = lambda *a: ps.enter_context(nc.sbuf_tensor(*a))
        Pb = lambda *a: ps.enter_context(nc.psum_tensor(*a))
        xq_fm = Sb("f1_xq_fm", [128, ND * S], BF16)
        r_xq = Res()
        h_sb = Sb("f1_h_sb", [128, NH1 * 512], BF16)
        r_h = [Res() for _ in range(NH1)]
        w1s = Sb("f1_w1s", [128, 4 * ND * 128], BF16)
        r_w1s = [Res() for _ in range(4)]
        w2s = Sb("f1_w2s", [128, 4 * D], BF16)
        r_w2s = [Res() for _ in range(4)]
        xf = Sb("f1_xf", [128, 2 * D], F32)
        r_xf = [Res(), Res()]
        xnf = Sb("f1_xnf", [128, D], F32)
        r_xnf = Res()
        qi = Sb("f1_qi", [128, D], I32)
        r_qi = Res()
        qtile = Sb("f1_qt", [128, D], BF16)
        r_qt = Res()
        xs_row = Sb("f1_xs_row", [1, S], F32)
        r_xsr = Res()
        xsc_rep = Sb("f1_xsc_rep", [128, S], F32)
        r_xscr = Res()
        xsc1 = Sb("f1_xsc1", [128, 4], F32)
        r_x1 = Res()
        b1c = Sb("f1_b1c", [128, NH1], F32)
        r_b1 = Res()
        b2r = Sb("f1_b2r", [128, D], F32)
        r_b2 = Res()
        bstg = Sb("f1_bstg", [128, 128], F32)
        r_bs = Res()
        brf = Sb("f1_brf", [1, D], F32)
        r_br = Res()
        Mx = Sb("f1_Mx", [128, 512], F32)
        Mn = Sb("f1_Mn", [128, 512], F32)
        r_Mx, r_Mn = Res(), Res()
        xs2 = Sb("f1_xs2", [128, 512], F32)
        r_xs2 = Res()
        inv2 = Sb("f1_inv2", [128, 512], F32)
        r_iv2 = Res()
        uscr = Sb("f1_uscr", [128, 512], F32)
        r_us = Res()
        qi2 = Sb("f1_qi2", [128, 512], I32)
        r_qi2 = Res()
        sc2 = Sb("f1_sc2", [128, 8], F32)
        r_sc2 = Res()
        wfac = Sb("f1_wfac", [128, 2], F32)
        r_wf = Res()
        onesr = Sb("f1_onesr", [1, 128], F32)
        r_or = Res()
        peps = [Pb("f1_peps0", [128, 512], BF16)]
        peps.append(peps[0])
        r_pp = [Res()]
        r_pp.append(r_pp[0])
        pepf = Pb("f1_pepf", [128, 128], F32)
        r_pf = Res()
        ps_h = [Pb("f1_ps_h0", [128, 512], F32), Pb("f1_ps_h1", [128, 512], F32)]
        r_ph = [Res(), Res()]
        ps_o = [Pb("f1_ps_o0", [128, 512], F32), Pb("f1_ps_o1", [128, 512], F32),
                Pb("f1_ps_o2", [128, 512], F32), Pb("f1_ps_o3", [128, 512], F32)]
        r_po = [Res() for _ in range(4)]

        em.bar()
        em.op("pool", lambda e: e.memset(onesr[:], 1.0), writes=[r_or])
        em.op("dve", lambda e: e.tensor_scalar(out=wfac[:], in0=wsv[:],
                                               scalar1=-1.0 / 127.0, scalar2=None,
                                               op0=ALU.mult, op1=ALU.bypass),
              reads=[r_wsv], writes=[r_wf])
        em.dma(("sp", "q0"), bstg[0:NH1, 0:128],
               dr["b1"].ap().rearrange("(c p) -> c p", p=128), writes=[r_bs])
        em.op("pe", lambda e: e.transpose(pepf[:, 0:NH1], bstg[0:NH1, 0:128],
                                          consts["idn_f32"][0][0:NH1, 0:NH1]),
              reads=[r_bs, consts["idn_f32"][1]], writes=[r_pf])
        em.op("dve", lambda e: e.tensor_copy(out=b1c[:], in_=pepf[:, 0:NH1]),
              reads=[r_pf], writes=[r_b1])
        em.dma(("sp", "q0"), brf[0:1, 0:D],
               dr["b2"].ap().rearrange("(a d) -> a d", a=1), writes=[r_br])
        for _pc in range(2):
            em.op("pe", lambda e, _pc=_pc: e.matmul(
                ps_o[0][:, 0:512], lhsT=onesr[0:1, 0:128],
                rhs=brf[0:1, _pc * 512:(_pc + 1) * 512], start=True, stop=True),
                reads=[r_br, r_or], writes=[r_po[0]])
            em.op("dve", lambda e, _pc=_pc: e.tensor_copy(
                out=b2r[:, _pc * 512:(_pc + 1) * 512], in_=ps_o[0][:, 0:512]),
                reads=[r_po[0]], writes=[r_b2])
        for t in range(NT):
            sl = t % 2
            em.dma(("sp", "q0"), xf[:, sl * D: sl * D + D],
                   dr["xsave"].ap()[t * 128:(t + 1) * 128, :], writes=[r_xf[sl]])
            em.op("dve", lambda e, t=t, sl=sl: e.tensor_scalar(
                out=xnf[:, 0:D], in0=xf[:, sl * D: sl * D + D],
                scalar1=stats_m[:, t:t + 1], scalar2=stats_r[:, t:t + 1],
                op0=ALU.subtract, op1=ALU.mult),
                reads=[r_xf[sl], r_stats], writes=[r_xnf])
            em.op("dve", lambda e: e.tensor_reduce(
                out=xsc1[:, 0:1], in_=xnf[:, 0:D], axis=AX.X, op=ALU.max,
                apply_absolute_value=True), reads=[r_xnf], writes=[r_x1])
            em.op("dve", lambda e: e.tensor_scalar(
                out=xsc1[:, 1:2], in0=xsc1[:, 0:1], scalar1=1e-5, scalar2=None,
                op0=ALU.max, op1=ALU.bypass), reads=[r_x1], writes=[r_x1])
            em.op("dve", lambda e: e.reciprocal(out=xsc1[:, 2:3], in_=xsc1[:, 1:2]),
                  reads=[r_x1], writes=[r_x1])
            em.op("dve", lambda e: e.tensor_scalar(
                out=xsc1[:, 3:4], in0=xsc1[:, 2:3], scalar1=127.0, scalar2=None,
                op0=ALU.mult, op1=ALU.bypass), reads=[r_x1], writes=[r_x1])
            em.op("dve", lambda e: e.tensor_scalar(
                out=qi[:, 0:D], in0=xnf[:, 0:D], scalar1=xsc1[:, 3:4], scalar2=None,
                op0=ALU.mult, op1=ALU.bypass), reads=[r_xnf, r_x1], writes=[r_qi])
            em.op("dve", lambda e: e.tensor_copy(out=qtile[:, 0:D], in_=qi[:, 0:D]),
                  reads=[r_qi], writes=[r_qt])
            xqf3 = xq_fm[:].rearrange("p (c s) -> p c s", c=ND)
            for g in range(2):
                emit_transpose_batch4(
                    em, consts,
                    [qtile[:, (g * 4 + i) * 128:(g * 4 + i + 1) * 128] for i in range(4)],
                    xqf3[:, g * 4: g * 4 + 4, t * 128:(t + 1) * 128],
                    peps[g], [r_qt] * 4, r_xq, r_pp[g])
            em.op("pe", lambda e: e.transpose(pepf[0:1, 0:128], xsc1[:, 1:2],
                                              consts["idn_f32"][0][:]),
                  reads=[r_x1, consts["idn_f32"][1]], writes=[r_pf])
            em.op("dve", lambda e, t=t: e.tensor_copy(
                out=xs_row[0:1, t * 128:(t + 1) * 128], in_=pepf[0:1, 0:128]),
                reads=[r_pf], writes=[r_xsr])
        em.op("dve", lambda e: e.tensor_scalar(
            out=xs_row[0:1, :], in0=xs_row[0:1, :], scalar1=wfac[0:1, 0:1], scalar2=None,
            op0=ALU.mult, op1=ALU.bypass), reads=[r_xsr, r_wf], writes=[r_xsr])
        for _pc in range(4):
            em.op("pe", lambda e, _pc=_pc: e.matmul(
                ps_o[0][:, 0:512], lhsT=onesr[0:1, 0:128],
                rhs=xs_row[0:1, _pc * 512:(_pc + 1) * 512], start=True, stop=True),
                reads=[r_xsr, r_or], writes=[r_po[0]])
            em.op("dve", lambda e, _pc=_pc: e.tensor_copy(
                out=xsc_rep[:, _pc * 512:(_pc + 1) * 512], in_=ps_o[0][:, 0:512]),
                reads=[r_po[0]], writes=[r_xscr])
        em.bar()
        if PHASE_LIMIT <= 14:
            return
        for sb in range(4):
            for hc in range(NH1):
                sl = hc % 4
                em.dma(("sp", "q0" if hc % 2 == 0 else "q1"),
                       w1s[:, sl * ND * 128:(sl + 1) * ND * 128],
                       dr["W1q"].ap()[hc * ND * 128:(hc + 1) * ND * 128, :]
                       .rearrange("(c p) m -> p c m", p=128), writes=[r_w1s[sl]])
                php, rph = ps_h[hc % 2], r_ph[hc % 2]
                for dc in range(ND):
                    em.op("pe", lambda e, hc=hc, dc=dc, sb=sb, sl=sl, php=php: e.matmul(
                        php[:, 0:512],
                        lhsT=w1s[:, (sl * ND + dc) * 128:(sl * ND + dc + 1) * 128],
                        rhs=xq_fm[:, dc * S + sb * 512: dc * S + sb * 512 + 512],
                        start=(dc == 0), stop=(dc == ND - 1)),
                        reads=[r_w1s[sl], r_xq], writes=[rph])
                em.op("dve", lambda e, sb=sb, php=php: e.tensor_tensor(
                    out=uscr[:, 0:512], in0=php[:, 0:512],
                    in1=xsc_rep[:, sb * 512:(sb + 1) * 512], op=ALU.mult),
                    reads=[rph, r_xscr], writes=[r_us])
                em.op("act", lambda e, hc=hc: e.activation(
                    out=h_sb[:, hc * 512:(hc + 1) * 512], in_=uscr[:, 0:512],
                    func=AF.Gelu, bias=b1c[:, hc:hc + 1]),
                    reads=[r_us, r_b1], writes=[r_h[hc]])
                if hc == 0:
                    em.op("dve", lambda e: e.tensor_copy(out=Mx[:], in_=h_sb[:, 0:512]),
                          reads=[r_h[0]], writes=[r_Mx])
                    em.op("dve", lambda e: e.tensor_copy(out=Mn[:], in_=h_sb[:, 0:512]),
                          reads=[r_h[0]], writes=[r_Mn])
                else:
                    em.op("dve", lambda e, hc=hc: e.tensor_tensor(
                        out=Mx[:], in0=Mx[:], in1=h_sb[:, hc * 512:(hc + 1) * 512],
                        op=ALU.max), reads=[r_Mx, r_h[hc]], writes=[r_Mx])
                    em.op("dve", lambda e, hc=hc: e.tensor_tensor(
                        out=Mn[:], in0=Mn[:], in1=h_sb[:, hc * 512:(hc + 1) * 512],
                        op=ALU.min), reads=[r_Mn, r_h[hc]], writes=[r_Mn])
            em.op("dve", lambda e: e.tensor_scalar(
                out=Mn[:], in0=Mn[:], scalar1=-1.0, scalar2=None, op0=ALU.mult,
                op1=ALU.bypass), reads=[r_Mn], writes=[r_Mn])
            em.op("dve", lambda e: e.tensor_tensor(out=xs2[:], in0=Mx[:], in1=Mn[:],
                                                   op=ALU.max),
                  reads=[r_Mx, r_Mn], writes=[r_xs2])
            em.op("dve", lambda e: e.tensor_scalar(
                out=xs2[:], in0=xs2[:], scalar1=1e-5, scalar2=None, op0=ALU.max,
                op1=ALU.bypass), reads=[r_xs2], writes=[r_xs2])
            for st in range(4):
                em.op("pe", lambda e, st=st: e.transpose(
                    pepf[:, 0:128], xs2[:, st * 128:(st + 1) * 128],
                    consts["idn_f32"][0][:]),
                    reads=[r_xs2, consts["idn_f32"][1]], writes=[r_pf])
                em.op("dve", lambda e, st=st: e.tensor_reduce(
                    out=sc2[:, 4 + st: 5 + st], in_=pepf[:, 0:128], axis=AX.X,
                    op=ALU.max), reads=[r_pf], writes=[r_sc2])
                em.op("dve", lambda e, st=st: e.tensor_scalar(
                    out=sc2[:, st:st + 1], in0=sc2[:, 4 + st: 5 + st],
                    scalar1=wfac[:, 1:2], scalar2=None, op0=ALU.mult, op1=ALU.bypass),
                    reads=[r_sc2, r_wf], writes=[r_sc2])
                em.op("dve", lambda e, st=st: e.reciprocal(
                    out=sc2[:, 4 + st: 5 + st], in_=sc2[:, 4 + st: 5 + st]),
                    reads=[r_sc2], writes=[r_sc2])
                em.op("pe", lambda e, st=st: e.transpose(
                    pepf[0:1, 0:128], sc2[:, 4 + st: 5 + st],
                    consts["idn_f32"][0][:]),
                    reads=[r_sc2, consts["idn_f32"][1]], writes=[r_pf])
                em.op("dve", lambda e, st=st: e.tensor_scalar(
                    out=xs_row[0:1, st * 128:(st + 1) * 128], in0=pepf[0:1, 0:128],
                    scalar1=127.0, scalar2=None, op0=ALU.mult, op1=ALU.bypass),
                    reads=[r_pf], writes=[r_xsr])
            em.op("pe", lambda e: e.matmul(
                ps_o[0][:, 0:512], lhsT=onesr[0:1, 0:128], rhs=xs_row[0:1, 0:512],
                start=True, stop=True), reads=[r_xsr, r_or], writes=[r_po[0]])
            em.op("dve", lambda e: e.tensor_copy(out=inv2[:], in_=ps_o[0][:, 0:512]),
                  reads=[r_po[0]], writes=[r_iv2])
            em.op("dve", lambda e: e.tensor_copy(out=xs2[:], in_=inv2[:]),
                  reads=[r_iv2], writes=[r_xs2])
            for hc in range(NH1):
                em.op("dve", lambda e, hc=hc: e.tensor_tensor(
                    out=uscr[:, 0:512], in0=h_sb[:, hc * 512:(hc + 1) * 512],
                    in1=xs2[:], op=ALU.mult), reads=[r_h[hc], r_xs2], writes=[r_us])
                em.op("dve", lambda e: e.tensor_copy(out=qi2[:], in_=uscr[:, 0:512]),
                      reads=[r_us], writes=[r_qi2])
                em.op("dve", lambda e, hc=hc: e.tensor_copy(
                    out=h_sb[:, hc * 512:(hc + 1) * 512], in_=qi2[:]),
                    reads=[r_qi2], writes=[r_h[hc]])
            for db in range(2):
                for hc in range(NH1):
                    sl = hc % 4
                    em.dma(("sp", "q0" if hc % 2 == 0 else "q1"),
                           w2s[:, sl * D: sl * D + 512],
                           dr["W2q"].ap()[hc * 128:(hc + 1) * 128,
                                          db * 512:(db + 1) * 512],
                           writes=[r_w2s[sl]])
                    for st in range(4):
                        em.op("pe", lambda e, hc=hc, st=st, sl=sl: e.matmul(
                            ps_o[st][:, 0:512],
                            lhsT=h_sb[:, hc * 512 + st * 128: hc * 512 + (st + 1) * 128],
                            rhs=w2s[:, sl * D: sl * D + 512],
                            start=(hc == 0), stop=(hc == NH1 - 1)),
                            reads=[r_h[hc], r_w2s[sl]], writes=[r_po[st]])
                for st in range(4):
                    qt = sb * 4 + st
                    em.op("dve", lambda e, qt=qt, st=st, db=db: e.scalar_tensor_tensor(
                        out=uscr[:, 0:512], in0=ps_o[st][:, 0:512],
                        scalar=sc2[:, st:st + 1],
                        in1=b2r[:, db * 512:(db + 1) * 512],
                        op0=ALU.mult, op1=ALU.add),
                        reads=[r_po[st], r_sc2, r_b2], writes=[r_us])
                    em.op("dve", lambda e, qt=qt, db=db: e.tensor_tensor(
                        out=x_tm[:, qt * D + db * 512: qt * D + db * 512 + 512],
                        in0=x_tm[:, qt * D + db * 512: qt * D + db * 512 + 512],
                        in1=uscr[:, 0:512], op=ALU.add),
                        reads=[r_us, rx[qt]], writes=[rx[qt]])
        em.bar()



_LAYERS_ONLY = None
_SPLIT = False


def kernel(L0_attn_x, L0_attn_W, L0_attn_b, L0_ffn_w_gate, L0_ffn_w_up, L0_ffn_w_down,
           L1_attn_Wqkv, L1_attn_bqkv, L1_attn_Wout, L1_attn_bout,
           L1_ffn_W1, L1_ffn_b1, L1_ffn_W2, L1_ffn_b2):
    x = np.ascontiguousarray(np.asarray(L0_attn_x, dtype=np.float32))
    B = x.shape[0]
    w = {
        "W0": np.ascontiguousarray(np.asarray(L0_attn_W, np.float32)),
        "b0": np.ascontiguousarray(np.asarray(L0_attn_b, np.float32)),
        "wg": np.ascontiguousarray(np.asarray(L0_ffn_w_gate, np.float32)),
        "wu": np.ascontiguousarray(np.asarray(L0_ffn_w_up, np.float32)),
        "wd": np.ascontiguousarray(np.asarray(L0_ffn_w_down, np.float32)),
        "Wqkv": np.ascontiguousarray(np.asarray(L1_attn_Wqkv, np.float32)),
        "bqkv": np.ascontiguousarray(np.asarray(L1_attn_bqkv, np.float32)),
        "Wout": np.ascontiguousarray(np.asarray(L1_attn_Wout, np.float32)),
        "bout": np.ascontiguousarray(np.asarray(L1_attn_bout, np.float32)),
        "W1T": np.ascontiguousarray(np.asarray(L1_ffn_W1, np.float32).T),
        "b1": np.ascontiguousarray(np.asarray(L1_ffn_b1, np.float32)),
        "W2T": np.ascontiguousarray(np.asarray(L1_ffn_W2, np.float32).T),
        "b2": np.ascontiguousarray(np.asarray(L1_ffn_b2, np.float32)),
    }
    l0_names = ["W0", "b0", "wg", "wu", "wd"]
    l1_names = ["Wqkv", "bqkv", "Wout", "bout", "W1T", "b1", "W2T", "b2"]
    if _SPLIT:
        nc0 = build_program((0,))
        maps = [dict({"x_in": np.ascontiguousarray(x[c])}, **{n: w[n] for n in l0_names})
                for c in range(B)]
        res0 = run_bass_kernel_spmd(nc0, maps, core_ids=list(range(B)))
        x1 = np.stack([np.asarray(res0.results[c]["x_out"]) for c in range(B)])
        if _LAYERS_ONLY == (0,):
            return x1
        nc1 = build_program((1,))
        maps = [dict({"x_in": np.ascontiguousarray(x1[c])}, **{n: w[n] for n in l1_names})
                for c in range(B)]
        res1 = run_bass_kernel_spmd(nc1, maps, core_ids=list(range(B)))
        return np.stack([np.asarray(res1.results[c]["x_out"]) for c in range(B)])
    nc01 = build_program((0, 1))
    maps = [dict({"x_in": np.ascontiguousarray(x[c])},
                 **{n: w[n] for n in l0_names + l1_names}) for c in range(B)]
    res = run_bass_kernel_spmd(nc01, maps, core_ids=list(range(B)))
    return np.stack([np.asarray(res.results[c]["x_out"]) for c in range(B)])
